# version 02
# baseline (speedup 1.0000x reference)
import contextlib

import numpy as np
import ml_dtypes

import concourse.bass as bass
import concourse.mybir as mybir
from concourse.bass_utils import run_bass_kernel_spmd
from concourse.alu_op_type import AluOpType as AL

F32 = mybir.dt.float32
BF16 = mybir.dt.bfloat16
F16 = mybir.dt.float16
I8 = mybir.dt.int8
AF = mybir.ActivationFunctionType
BFD = ml_dtypes.bfloat16

B, T, H = 4, 2048, 2048
HALF = 1024
NI0, NI1 = 64, 44
ROPE_SCALE = 128 ** -0.5


def _tile_w(W, nct, ncp):
    return np.ascontiguousarray(
        np.asarray(W).reshape(nct, 128, ncp, 128).transpose(2, 1, 0, 3))


def _vec_pack(v, n):
    return np.ascontiguousarray(np.asarray(v, np.float32).reshape(n, 128).T)


def _host_inputs(I):
    x = np.asarray(I["L0_attn_hidden"], np.float32)
    sh = {}
    sh["wk0"] = _tile_w(I["L0_attn_Wk"], 16, 16).astype(np.float32)
    sh["wv0"] = _tile_w(I["L0_attn_Wv"], 16, 16).astype(np.float32)
    sh["wr0"] = _tile_w(I["L0_attn_Wr"], 16, 16).astype(np.float32)
    sh["wo0"] = _tile_w(I["L0_attn_Wo"], 16, 16).astype(np.float32)
    g0 = _tile_w(I["L0_ffn_w_gate"], 16, NI0)
    u0 = _tile_w(I["L0_ffn_w_up"], 16, NI0)
    sh["gu0"] = np.ascontiguousarray(np.concatenate([g0, u0], axis=2)).astype(np.int8)
    sh["d0"] = np.ascontiguousarray(_tile_w(I["L0_ffn_w_down"], NI0, 16)).astype(np.int8)
    sh["wq1"] = _tile_w(I["L1_attn_Wq"], 16, 16).astype(np.float32)
    sh["wk1"] = _tile_w(I["L1_attn_Wk"], 16, 4).astype(np.float32)
    sh["wv1"] = np.ascontiguousarray(
        np.asarray(I["L1_attn_Wv"], np.float32).reshape(16, 128, 512))
    sh["wo1"] = _tile_w(I["L1_attn_Wo"], 16, 16).astype(np.float32)
    g1 = _tile_w(np.asarray(I["L1_ffn_gate_w"], np.float32).T, 16, NI1)
    u1 = _tile_w(np.asarray(I["L1_ffn_up_w"], np.float32).T, 16, NI1)
    sh["gu1"] = np.ascontiguousarray(np.concatenate([g1, u1], axis=2)).astype(np.float32)
    sh["d1"] = np.ascontiguousarray(
        _tile_w(np.asarray(I["L1_ffn_down_w"], np.float32).T, NI1, 16)).astype(np.float32)

    vp = np.stack([
        _vec_pack(I["L0_attn_time_decay"], 16),
        _vec_pack(I["L0_attn_time_first"], 16),
        _vec_pack(np.asarray(I["L0_attn_time_mix_key"]).reshape(-1), 16),
        _vec_pack(np.asarray(I["L0_attn_time_mix_value"]).reshape(-1), 16),
        _vec_pack(np.asarray(I["L0_attn_time_mix_receptance"]).reshape(-1), 16),
    ], axis=1)
    sh["vpk"] = np.ascontiguousarray(vp.reshape(128, 80))
    sh["sg0"] = _vec_pack(I["L0_ffn_s_gate"], NI0)
    sh["su0"] = _vec_pack(I["L0_ffn_s_up"], NI0)
    sh["sd0"] = _vec_pack(I["L0_ffn_s_down"], 16)
    sh["gs1"] = _vec_pack(I["L1_ffn_gate_s"], NI1)
    sh["us1"] = _vec_pack(I["L1_ffn_up_s"], NI1)
    sh["ds1"] = _vec_pack(I["L1_ffn_down_s"], 16)
    sh["onesf"] = np.ones((128, 1), np.float32)
    sh["onesrow"] = np.ones((1, 128), np.float32)
    sh["onesbf"] = np.ones((128, 1), BFD)
    sh["perm"] = np.ascontiguousarray(np.roll(np.eye(128, dtype=np.float32), 64, axis=0))

    t_ = np.arange(T, dtype=np.float32)
    inv_freq = (1.0 / 10000.0 ** (np.arange(0, 128, 2, dtype=np.float32) / 128)).astype(np.float32)
    freqs = t_[:, None] * inv_freq[None, :]
    cos_t = np.cos(freqs).astype(np.float32)
    sin_t = np.sin(freqs).astype(np.float32)
    tri = (np.arange(512)[:, None] <= np.arange(512)[None, :]).astype(np.float32)

    maps = []
    for c in range(8):
        b, s = c // 2, c % 2
        t0 = s * HALF
        m = dict(sh)
        xo = np.zeros((HALF + 1, H), np.float32)
        if t0 > 0:
            xo[0] = x[b, t0 - 1]
        xo[1:] = x[b, t0:t0 + HALF]
        m["x_fm"] = np.ascontiguousarray(xo.T)
        m["smask"] = np.full((128, 1), float(s), np.float32)
        cs = cos_t[t0:t0 + HALF].T
        sn = sin_t[t0:t0 + HALF].T
        m["cosq"] = np.ascontiguousarray(np.vstack([cs, cs]))
        m["sinq"] = np.ascontiguousarray(np.vstack([-sn, sn]))
        bm = np.zeros((8, 128, 512), np.float32)
        for j in range(4):
            bm[j] = tri[j * 128:(j + 1) * 128, :] if s == 0 else 1.0
            bm[4 + j] = 0.0 if s == 0 else tri[j * 128:(j + 1) * 128, :]
        m["bandmask"] = np.ascontiguousarray(
            bm.transpose(1, 0, 2).reshape(128, 8 * 512)).astype(BFD)
        bt = np.zeros((2, 16), np.float32)
        for qt in range(2):
            band = {4 * qt + j for j in range(4)} | {8 + 4 * qt + j for j in range(4)}
            for kt in range(16):
                if kt in band or kt < 8 * s + 4 * qt:
                    bt[qt, kt] = 0.0
                else:
                    bt[qt, kt] = -1e9
        m["biastbl"] = np.ascontiguousarray(
            np.broadcast_to(bt.reshape(1, 32), (128, 32)).astype(np.float32))
        maps.append(m)
    return maps


class _Sem:
    __slots__ = ("h", "n")

    def __init__(self, es, nc, name):
        self.h = es.enter_context(nc.semaphore(name))
        self.n = 0


class _Slots:
    def __init__(self, n):
        self.gate = [None] * n
        self.i = 0

    def acq(self, sched, eng):
        s = self.i % len(self.gate)
        self.i += 1
        g = self.gate[s]
        if g is not None:
            _w(sched, eng, g[0], g[1])
        return s

    def rel(self, s, sem):
        self.gate[s] = (sem, sem.n)


def _w(sched, eng, sem, val=None):
    v = sem.n if val is None else val
    if v > 0:
        sched[eng].append(lambda e, h=sem.h, v=v: e.wait_ge(h, v))


def _op(sched, eng, fn, sem=None, k=1):
    if sem is not None:
        sem.n += k
        sched[eng].append(lambda e, fn=fn, h=sem.h, k=k: fn(e).then_inc(h, k))
    else:
        sched[eng].append(lambda e, fn=fn: fn(e))


def build_nc(stage=6):
    nocc = stage == 21
    if nocc:
        stage = 2
    nc = bass.Bass("TRN2", target_bir_lowering=False, debug=False)
    es = contextlib.ExitStack()

    def din(name, shape, dt):
        return nc.dram_tensor(name, shape, dt, kind="ExternalInput").ap()

    x_fm = din("x_fm", [2048, 1025], F32)
    wk0 = din("wk0", [16, 128, 16, 128], F32)
    wv0 = din("wv0", [16, 128, 16, 128], F32)
    wr0 = din("wr0", [16, 128, 16, 128], F32)
    wo0 = din("wo0", [16, 128, 16, 128], F32)
    gu0 = din("gu0", [NI0, 128, 32, 128], I8)
    d0 = din("d0", [16, 128, NI0, 128], I8)
    wq1 = din("wq1", [16, 128, 16, 128], F32)
    wk1 = din("wk1", [4, 128, 16, 128], F32)
    wv1 = din("wv1", [16, 128, 512], F32)
    wo1 = din("wo1", [16, 128, 16, 128], F32)
    gu1 = din("gu1", [NI1, 128, 32, 128], F32)
    d1 = din("d1", [16, 128, NI1, 128], F32)
    vpk = din("vpk", [128, 80], F32)
    sg0 = din("sg0", [128, NI0], F32)
    su0 = din("su0", [128, NI0], F32)
    sd0 = din("sd0", [128, 16], F32)
    gs1 = din("gs1", [128, NI1], F32)
    us1 = din("us1", [128, NI1], F32)
    ds1 = din("ds1", [128, 16], F32)
    onesf = din("onesf", [128, 1], F32)
    onesrow = din("onesrow", [1, 128], F32)
    onesbf = din("onesbf", [128, 1], BF16)
    permi = din("perm", [128, 128], F32)
    smask = din("smask", [128, 1], F32)
    cosq = din("cosq", [128, 1024], F32)
    sinq = din("sinq", [128, 1024], F32)
    bandmask = din("bandmask", [128, 8 * 512], BF16)
    biastbl = din("biastbl", [128, 32], F32)

    out_fm = nc.dram_tensor("out_fm", [2048, 1024], F32, kind="ExternalOutput").ap()

    cc1_in = nc.dram_tensor("cc1_in", [128, 32], F32, kind="Internal").ap()
    cc1_out = nc.dram_tensor("cc1_out", [2, 128, 32], F32, kind="Internal").ap()
    cc2_in = nc.dram_tensor("cc2_in", [1048576], BF16, kind="Internal").ap()
    cc2_out = nc.dram_tensor("cc2_out", [2, 1048576], BF16, kind="Internal").ap()
    cc2K_in = cc2_in[0:524288].rearrange("(h p n) -> h p n", h=4, p=128)
    cc2V_in = cc2_in[524288:1048576].rearrange("(t p n) -> t p n", t=8, p=128)

    dbg = {}
    if stage < 6:
        def dbgt(name, shape):
            dbg[name] = nc.dram_tensor(name, shape, F32, kind="ExternalOutput").ap()
        dbgt("dbg_xn", [16, 128, 1025])
        dbgt("dbg_k", [16, 128, 1024])
        dbgt("dbg_v", [16, 128, 1024])
        if stage >= 2:
            dbgt("dbg_ai", [16, 128, 1024])
            dbgt("dbg_state", [128, 32])
        if stage >= 3:
            dbgt("dbg_x1", [16, 128, 1024])
        if stage >= 4:
            dbgt("dbg_xn1", [16, 128, 1024])
            dbgt("dbg_kf", [4, 128, 2048])
            dbgt("dbg_vf", [128, 2048])
        if stage >= 5:
            dbgt("dbg_on", [16, 128, 1024])

    ARENA = 207136
    _ar = nc.alloc_sbuf_tensor("arena", [128, ARENA], I8)
    _base = nc.lookup_mloc(_ar).addr

    def sb(name, shape, dt, off):
        return nc.alloc_sbuf_tensor_at(name, shape, dt, offset=_base + off).ap()

    XN, XR, AI, SP, SCb = 0, 33280, 98816, 132096, 152896
    xn = [sb(f"xn{i}", [128, 1025], BF16, XN + i * 2080) for i in range(16)]
    xr = [sb(f"xr{i}", [128, 1024], F32, XR + i * 4096) for i in range(16)]
    kcar = [sb(f"kc{i}", [128, 1024], F16, XR + i * 4096) for i in range(16)]
    vcar = [sb(f"vc{i}", [128, 1024], F16, XR + i * 4096 + 2048) for i in range(16)]
    ai = [sb(f"ai{i}", [128, 1024], BF16, AI + i * 2080) for i in range(16)]
    fuse = [sb(f"fu{i}", [128, 512], BF16, AI + i * 1024) for i in range(32)]
    xa = sb("xa", [128, 1025], F32, SP + 0)
    xb = sb("xb", [128, 1025], F32, SP + 4160)
    xsq = sb("xsq", [128, 1025], F32, SP + 8320)
    rstdB = sb("rstdB", [128, 1025], F32, SP + 12480)
    mrsB = sb("mrsB", [128, 1025], F32, SP + 16640)
    mixp = [sb(f"mx{i}", [128, 512], BF16, SP + i * 1024) for i in range(16)]
    atile = sb("atile", [128, 512], F32, SP + 16704)
    wt = sb("wt", [128, 1025], F32, SP + 0)
    qs = sb("qs", [128, 1025], F32, SP + 4160)
    psc = sb("psc", [128, 1025], F32, SP + 8320)
    NUMt = sb("NUMt", [128, 1025], F32, SP + 12480)
    DENt = sb("DENt", [128, 1025], F32, SP + 16640)
    wv1sb = sb("wv1sb", [128, 16 * 512], BF16, SP + 0)
    vstg = [sb(f"vst{i}", [128, 512], BF16, SP + 16384 + i * 1024) for i in range(2)]
    kstg = [sb(f"kst{i}", [128, 512], BF16, SP + 18432 + i * 1024) for i in range(2)]

    o = SCb
    def nx(sz):
        nonlocal o
        r = o
        o += sz
        return r
    tdp = sb("tdp", [128, 16], F32, nx(64))
    tfp = sb("tfp", [128, 16], F32, nx(64))
    tmkp = sb("tmkp", [128, 16], F32, nx(64))
    tmvp = sb("tmvp", [128, 16], F32, nx(64))
    tmrp = sb("tmrp", [128, 16], F32, nx(64))
    etfp = sb("etfp", [128, 16], F32, nx(64))
    smk = sb("smk", [128, 1], F32, nx(32))
    sg0t = sb("sg0t", [128, NI0], F32, nx(256))
    su0t = sb("su0t", [128, NI0], F32, nx(256))
    sd0t = sb("sd0t", [128, 16], F32, nx(64))
    gs1t = sb("gs1t", [128, NI1], F32, nx(192))
    us1t = sb("us1t", [128, NI1], F32, nx(192))
    ds1t = sb("ds1t", [128, 16], F32, nx(64))
    onf = sb("onf", [128, 1], F32, nx(32))
    onrow = sb("onrow", [1, 128], F32, nx(512))
    onbf = sb("onbf", [128, 1], BF16, nx(32))
    prm = sb("prm", [128, 128], F32, nx(512))
    btbl = sb("btbl", [128, 32], F32, nx(128))
    bmsk = sb("bmsk", [128, 8 * 512], BF16, nx(8192))
    cosT = sb("cosT", [128, 1024], F32, nx(4096))
    sinT = sb("sinT", [128, 1024], F32, nx(4096))
    stpk = sb("stpk", [128, 32], F32, nx(128))
    rcvs = sb("rcvs", [128, 32], F32, nx(128))
    mrcv = sb("mrcv", [128, 32], F32, nx(128))
    lrec = sb("lrec", [1, 512], F32, nx(2048))
    lscr = sb("lscr", [1, 512], F32, nx(2048))
    t1 = sb("t1", [128, 512], F32, nx(2048))
    t2 = sb("t2", [128, 512], F32, nx(2048))
    st1 = sb("st1", [1, 1025], F32, AI + 0)
    st2 = sb("st2", [1, 1025], F32, AI + 4160)
    st3 = sb("st3", [1, 1025], F32, AI + 8320)
    gsil = [sb(f"gsil{i}", [128, 512], BF16, nx(1024)) for i in range(2)]
    kkv = sb("kkv", [128, 2048], BF16, SP + 0)
    vkv = sb("vkv", [128, 2048], BF16, SP + 4096)
    qbf = [sb(f"qbf{i}", [128, 1024], BF16, SP + 8192 + i * 2048) for i in range(2)]
    ptb = [sb(f"ptb{i}", [128, 512], BF16, SP + 12288 + i * 1024) for i in range(2)]
    lrecB = sb("lrecB", [128, 512], F32, SP + 14336)
    pbuf = [sb(f"pbW{i}", [128, 2048], BF16, nx(4096)) for i in range(2)]
    dbuf = [sb(f"dbW{i}", [128, 4096], BF16, nx(8192)) for i in range(2)]
    assert o <= ARENA, o

    pbank = [es.enter_context(nc.psum_tensor(f"ps{i}", [128, 512], F32))
             for i in range(8)]

    sched = {"t": [], "v": [], "s": [], "g": [], "y": []}
    sx = _Sem(es, nc, "sx")
    sw = _Sem(es, nc, "sw")
    smm = _Sem(es, nc, "smm")
    sv = _Sem(es, nc, "sv")
    ssc = _Sem(es, nc, "ssc")
    sgd = _Sem(es, nc, "sgd")
    scc = _Sem(es, nc, "scc")
    sci = _Sem(es, nc, "sci")
    sco = _Sem(es, nc, "sco")
    sout = _Sem(es, nc, "sout")

    def W(eng, sem, val=None):
        _w(sched, eng, sem, val)

    def E(eng, fn, sem=None, k=1):
        _op(sched, eng, fn, sem, k)

    def joinall(*engs):
        for eng in engs:
            for s_ in (smm, sv, ssc, sx, sw, sgd, sci, sco, scc):
                W(eng, s_)

    def finish():
        with nc.Block() as blk:
            @blk.sync
            def _(e):
                for f in sched["y"]:
                    f(e)

            @blk.gpsimd
            def _(e):
                for f in sched["g"]:
                    f(e)

            @blk.tensor
            def _(e):
                for f in sched["t"]:
                    f(e)

            @blk.vector
            def _(e):
                for f in sched["v"]:
                    f(e)

            @blk.scalar
            def _(e):
                for f in sched["s"]:
                    f(e)
        return nc, es, dbg

    for dst, src in [(tdp, vpk[:, 0:16]), (tfp, vpk[:, 16:32]), (tmkp, vpk[:, 32:48]),
                     (tmvp, vpk[:, 48:64]), (tmrp, vpk[:, 64:80]),
                     (smk, smask), (sg0t, sg0), (su0t, su0), (sd0t, sd0),
                     (gs1t, gs1), (us1t, us1), (ds1t, ds1), (onf, onesf),
                     (onrow, onesrow), (onbf, onesbf), (prm, permi), (btbl, biastbl),
                     (bmsk, bandmask), (cosT, cosq), (sinT, sinq)]:
        E("y", lambda e, d=dst, s=src: e.dma_start(out=d[:], in_=s), sx, 16)
    n_const = sx.n
    W("s", sx, n_const)
    E("s", lambda e: e.activation(tdp[:], tdp[:], AF.Exp), ssc)
    E("s", lambda e: e.activation(etfp[:], tfp[:], AF.Exp), ssc)
    W("v", ssc)
    E("v", lambda e: e.tensor_scalar_mul(tdp[:], tdp[:], -1.0), sv)
    n_td = sv.n

    chunks = [(0, 512), (512, 1024), (1024, 1025)]
    xld = _Slots(2)
    xbufs = [xa, xb]
    for ct in range(16):
        sl = xld.acq(sched, "y")
        E("y", lambda e, d=xbufs[sl], ct=ct: e.dma_start(
            out=d[:], in_=x_fm[ct * 128:(ct + 1) * 128, :]), sx, 16)
        W("s", sx)
        W("s", smm)
        E("s", lambda e, s_=xbufs[sl]: e.activation(xsq[:], s_[:], AF.Square), ssc)
        W("t", sx)
        for ci, (a, b_) in enumerate(chunks):
            E("t", lambda e, s_=xbufs[sl], a=a, b_=b_, ci=ci, ct=ct: e.matmul(
                pbank[ci][0:1, 0:b_ - a], onf[:], s_[:, a:b_],
                start=(ct == 0), stop=(ct == 15)), smm if ci == 2 else None)
        W("t", ssc)
        for ci, (a, b_) in enumerate(chunks):
            E("t", lambda e, a=a, b_=b_, ci=ci, ct=ct: e.matmul(
                pbank[3 + ci][0:1, 0:b_ - a], onf[:], xsq[:, a:b_],
                start=(ct == 0), stop=(ct == 15)), smm if ci == 2 else None)
        xld.rel(sl, smm)

    def emit_ln_lane(pb_sum, pb_sq, chs):
        n = chs[-1][1]
        W("v", smm)
        for idx, (a, b_) in enumerate(chs):
            w_ = b_ - a
            E("v", lambda e, a=a, w_=w_, i=pb_sum[idx]: e.tensor_scalar(
                st1[0:1, a:a + w_], pbank[i][0:1, 0:w_], 1.0 / 2048.0, None, AL.mult), sv)
            E("v", lambda e, a=a, w_=w_, i=pb_sq[idx]: e.tensor_scalar(
                st2[0:1, a:a + w_], pbank[i][0:1, 0:w_], 1.0 / 2048.0, None, AL.mult), sv)
        E("v", lambda e, n=n: e.tensor_tensor(st3[0:1, 0:n], st1[0:1, 0:n], st1[0:1, 0:n], AL.mult), sv)
        E("v", lambda e, n=n: e.tensor_tensor(st3[0:1, 0:n], st2[0:1, 0:n], st3[0:1, 0:n], AL.subtract), sv)
        E("v", lambda e, n=n: e.tensor_scalar(st3[0:1, 0:n], st3[0:1, 0:n], 1e-5, None, AL.add), sv)
        W("s", sv)
        E("s", lambda e, n=n: e.activation(st2[0:1, 0:n], st3[0:1, 0:n], AF.Ln), ssc)
        E("s", lambda e, n=n: e.activation(st3[0:1, 0:n], st2[0:1, 0:n], AF.Exp,
                                           scale=-0.5), ssc)
        W("v", ssc)
        E("v", lambda e, n=n: e.tensor_tensor(
            st1[0:1, 0:n], st1[0:1, 0:n], st3[0:1, 0:n], AL.mult), sv)
        W("t", sv)
        for dst, row in [(rstdB, st3), (mrsB, st1)]:
            for (a, b_) in chs:
                E("t", lambda e, a=a, b_=b_, row=row: e.matmul(
                    pbank[6][:, 0:b_ - a], onrow[:], row[0:1, a:b_],
                    start=True, stop=True), smm)
                W("s", smm)
                E("s", lambda e, a=a, b_=b_, dst=dst: e.copy(
                    dst[:, a:b_], pbank[6][:, 0:b_ - a]), ssc)
                W("t", ssc)

    emit_ln_lane([0, 1, 2], [3, 4, 5], chunks)
    for ct in range(16):
        sl = xld.acq(sched, "y")
        E("y", lambda e, d=xbufs[sl], ct=ct: e.dma_start(
            out=d[:], in_=x_fm[ct * 128:(ct + 1) * 128, :]), sx, 16)
        W("v", sx)
        W("v", ssc)
        E("v", lambda e, s_=xbufs[sl]: e.tensor_tensor(s_[:], s_[:], rstdB[:], AL.mult), sv)
        E("v", lambda e, s_=xbufs[sl], ct=ct: e.tensor_tensor(
            xn[ct][:], s_[:], mrsB[:], AL.subtract), sv)
        xld.rel(sl, sv)
    n_xn = sv.n

    pslot = _Slots(2)
    kvb = _Slots(4)
    W("v", sx, n_const)
    for tik in range(2):
        cur0, sh0 = 1 + tik * 512, tik * 512
        for wsrc, tmx, dstcar in [(wk0, tmkp, kcar), (wv0, tmvp, vcar)]:
            for ci in range(16):
                E("v", lambda e, ci=ci, cur0=cur0, sh0=sh0: e.tensor_tensor(
                    atile[:], xn[ci][:, cur0:cur0 + 512], xn[ci][:, sh0:sh0 + 512],
                    AL.subtract), sv)
                E("v", lambda e, ci=ci, sh0=sh0, tmx=tmx: e.scalar_tensor_tensor(
                    mixp[ci][:], atile[:], tmx[:, ci:ci + 1], xn[ci][:, sh0:sh0 + 512],
                    AL.mult, AL.add), sv)
            n_mix = sv.n
            for co in range(16):
                sl = pslot.acq(sched, "g")
                E("g", lambda e, sl=sl, w_=wsrc, co=co: e.dma_start(
                    out=pbuf[sl][:], in_=w_[co].rearrange("p a b -> p (a b)")), sw, 16)
                bk = kvb.acq(sched, "t")
                W("t", sw)
                W("t", sv, n_mix)
                for ci in range(16):
                    E("t", lambda e, sl=sl, ci=ci, bk=bk: e.matmul(
                        pbank[bk][:], pbuf[sl][:, ci * 128:(ci + 1) * 128], mixp[ci][:],
                        start=(ci == 0), stop=(ci == 15)), smm if ci == 15 else None)
                pslot.rel(sl, smm)
                W("s", smm)
                E("s", lambda e, bk=bk, co=co, tik=tik, d=dstcar: e.copy(
                    d[co][:, tik * 512:(tik + 1) * 512], pbank[bk][:]), ssc)
                kvb.rel(bk, ssc)
            W("v", smm)
    n_kv = ssc.n

    if stage < 6:
        joinall("g")
        for ct in range(16):
            E("g", lambda e, ct=ct: e.dma_start(out=dbg["dbg_xn"][ct], in_=xn[ct][:]), sout, 16)
            E("g", lambda e, ct=ct: e.dma_start(out=dbg["dbg_k"][ct], in_=kcar[ct][:]), sout, 16)
            E("g", lambda e, ct=ct: e.dma_start(out=dbg["dbg_v"][ct], in_=vcar[ct][:]), sout, 16)
    if stage == 1:
        W("y", sout, sout.n)
        E("y", lambda e: e.dma_start(out=out_fm[0:128, :], in_=xr[0][:]), sout, 16)
        W("y", sout, sout.n)
        return finish()

    E("v", lambda e: e.memset(wt[:, 0:1], 1.0), sv)
    E("v", lambda e: e.memset(qs[:, 0:1], 0.0), sv)
    E("v", lambda e: e.memset(psc[:, 0:1], 0.0), sv)
    W("s", sv, n_td)
    W("v", ssc, n_kv)
    for ct in range(16):
        W("s", sv)
        E("s", lambda e, ct=ct: e.activation(qs[:, 1:1025], kcar[ct][:], AF.Exp), ssc)
        E("s", lambda e, ct=ct: e.activation(
            wt[:, 1:1025], xn[0][:, 0:1024], AF.Exp, bias=tdp[:, ct:ct + 1], scale=0.0), ssc)
        W("v", ssc)
        E("v", lambda e, ct=ct: e.tensor_tensor(
            psc[:, 1:1025], qs[:, 1:1025], vcar[ct][:], AL.mult), sv)
        E("v", lambda e: e.tensor_tensor_scan(NUMt[:], wt[:], psc[:], 0.0, AL.mult, AL.add), sv)
        E("v", lambda e: e.tensor_tensor_scan(DENt[:], wt[:], qs[:], 0.0, AL.mult, AL.add), sv)
        E("v", lambda e, ct=ct: e.tensor_copy(stpk[:, ct:ct + 1], NUMt[:, 1024:1025]), sv)
        E("v", lambda e, ct=ct: e.tensor_copy(stpk[:, 16 + ct:17 + ct], DENt[:, 1024:1025]), sv)
    if nocc:
        E("v", lambda e: e.memset(mrcv[:], 0.0), sv)
    else:
        W("y", sv)
        E("y", lambda e: e.dma_start(out=cc1_in, in_=stpk[:]), sci, 16)
        W("g", sci)
        E("g", lambda e: e.collective_compute(
            "AllGather", AL.bypass, replica_groups=[[0, 1], [2, 3], [4, 5], [6, 7]],
            ins=[cc1_in], outs=[cc1_out]), scc)
        W("y", scc)
        E("y", lambda e: e.dma_start(out=rcvs[:], in_=cc1_out[0]), sco, 16)
        W("v", sco)
        E("v", lambda e: e.tensor_scalar_mul(mrcv[:], rcvs[:], smk[:, 0:1]), sv)

    rb = _Slots(2)
    for tik in range(2):
        cur0, sh0 = 1 + tik * 512, tik * 512
        W("v", ssc)
        for ci in range(16):
            E("v", lambda e, ci=ci, cur0=cur0, sh0=sh0: e.tensor_tensor(
                atile[:], xn[ci][:, cur0:cur0 + 512], xn[ci][:, sh0:sh0 + 512],
                AL.subtract), sv)
            E("v", lambda e, ci=ci, sh0=sh0: e.scalar_tensor_tensor(
                mixp[ci][:], atile[:], tmrp[:, ci:ci + 1], xn[ci][:, sh0:sh0 + 512],
                AL.mult, AL.add), sv)
        n_mix = sv.n
        for co in range(16):
            sl = pslot.acq(sched, "g")
            E("g", lambda e, sl=sl, co=co: e.dma_start(
                out=pbuf[sl][:], in_=wr0[co].rearrange("p a b -> p (a b)")), sw, 16)
            bk = rb.acq(sched, "t")
            W("t", sw)
            W("t", sv, n_mix)
            for ci in range(16):
                E("t", lambda e, sl=sl, ci=ci, bk=bk: e.matmul(
                    pbank[4 + bk][:], pbuf[sl][:, ci * 128:(ci + 1) * 128], mixp[ci][:],
                    start=(ci == 0), stop=(ci == 15)), smm if ci == 15 else None)
            pslot.rel(sl, smm)
            W("s", smm)
            E("s", lambda e, bk=bk, co=co, tik=tik: e.activation(
                ai[co][:, tik * 512:(tik + 1) * 512], pbank[4 + bk][:], AF.Sigmoid), ssc)
            rb.rel(bk, ssc)
        W("v", smm)
    n_r = ssc.n

    E("v", lambda e: e.memset(wt[:, 0:1], 1.0), sv)
    E("v", lambda e: e.memset(qs[:, 0:1], 0.0), sv)
    E("v", lambda e: e.memset(psc[:, 0:1], 0.0), sv)
    W("v", ssc, n_r)
    for ct in range(16):
        W("s", sv)
        E("s", lambda e, ct=ct: e.activation(qs[:, 1:1025], kcar[ct][:], AF.Exp), ssc)
        E("s", lambda e, ct=ct: e.activation(
            wt[:, 1:1025], xn[0][:, 0:1024], AF.Exp, bias=tdp[:, ct:ct + 1], scale=0.0), ssc)
        W("v", ssc)
        E("v", lambda e, ct=ct: e.tensor_tensor(
            psc[:, 1:1025], qs[:, 1:1025], vcar[ct][:], AL.mult), sv)
        E("v", lambda e, ct=ct: e.tensor_tensor_scan(
            NUMt[:], wt[:], psc[:], mrcv[:, ct:ct + 1], AL.mult, AL.add), sv)
        E("v", lambda e, ct=ct: e.tensor_tensor_scan(
            DENt[:], wt[:], qs[:], mrcv[:, 16 + ct:17 + ct], AL.mult, AL.add), sv)
        E("v", lambda e, ct=ct: e.scalar_tensor_tensor(
            NUMt[:, 0:1024], psc[:, 1:1025], etfp[:, ct:ct + 1], NUMt[:, 0:1024],
            AL.mult, AL.add), sv)
        E("v", lambda e, ct=ct: e.scalar_tensor_tensor(
            DENt[:, 0:1024], qs[:, 1:1025], etfp[:, ct:ct + 1], DENt[:, 0:1024],
            AL.mult, AL.add), sv)
        W("s", sv)
        E("s", lambda e: e.activation(wt[:, 1:1025], DENt[:, 0:1024], AF.Ln), ssc)
        E("s", lambda e: e.activation(
            DENt[:, 0:1024], wt[:, 1:1025], AF.Exp, scale=-1.0), ssc)
        W("v", ssc)
        E("v", lambda e: e.tensor_tensor(
            NUMt[:, 0:1024], NUMt[:, 0:1024], DENt[:, 0:1024], AL.mult), sv)
        E("v", lambda e, ct=ct: e.tensor_tensor(
            ai[ct][:], NUMt[:, 0:1024], ai[ct][:], AL.mult), sv)
        E("v", lambda e: e.memset(wt[:, 0:1], 1.0), sv)
    n_ai = sv.n

    if stage < 6:
        joinall("g")
        for ct in range(16):
            E("g", lambda e, ct=ct: e.dma_start(out=dbg["dbg_ai"][ct], in_=ai[ct][:]), sout, 16)
        E("g", lambda e: e.dma_start(out=dbg["dbg_state"], in_=mrcv[:]), sout, 16)
    if stage == 2:
        W("y", sout, sout.n)
        E("y", lambda e: e.dma_start(out=out_fm[0:128, :], in_=xr[0][:]), sout, 16)
        W("y", sout, sout.n)
        return finish()

    wob = _Slots(2)
    for co in range(16):
        sl = pslot.acq(sched, "g")
        E("g", lambda e, sl=sl, co=co: e.dma_start(
            out=pbuf[sl][:], in_=wo0[co].rearrange("p a b -> p (a b)")), sw, 16)
        W("t", sw)
        W("t", sv, n_ai)
        for tik in range(2):
            bk = wob.acq(sched, "t")
            for ct in range(16):
                E("t", lambda e, sl=sl, ct=ct, bk=bk, tik=tik: e.matmul(
                    pbank[bk][:], pbuf[sl][:, ct * 128:(ct + 1) * 128],
                    ai[ct][:, tik * 512:(tik + 1) * 512],
                    start=(ct == 0), stop=(ct == 15)), smm if ct == 15 else None)
            W("s", smm)
            E("s", lambda e, bk=bk, co=co, tik=tik: e.copy(
                xr[co][:, tik * 512:(tik + 1) * 512], pbank[bk][:]), ssc)
            wob.rel(bk, ssc)
        pslot.rel(sl, smm)
    n_wo_mm = smm.n
    n_wo = ssc.n

    joinall("t", "v", "s", "g")
    dslot = _Slots(2)
    gub = _Slots(4)
    dnb = _Slots(2)
    gsl = _Slots(2)
    for tik in range(2):
        for ig in range(2):
            for ipo in range(32):
                ip = ig * 32 + ipo
                sl = dslot.acq(sched, "g")
                E("g", lambda e, sl=sl, ip=ip: e.dma_start(
                    out=dbuf[sl][:], in_=gu0[ip].rearrange("p a b -> p (a b)")), sw, 16)
                bkg = gub.acq(sched, "t")
                W("t", sw)
                for ci in range(16):
                    E("t", lambda e, sl=sl, ci=ci, bkg=bkg, tik=tik: e.matmul(
                        pbank[bkg][:], dbuf[sl][:, ci * 128:(ci + 1) * 128],
                        xn[ci][:, 1 + tik * 512:1 + tik * 512 + 512],
                        start=(ci == 0), stop=(ci == 15)), smm if ci == 15 else None)
                bku = gub.acq(sched, "t")
                for ci in range(16):
                    E("t", lambda e, sl=sl, ci=ci, bku=bku, tik=tik: e.matmul(
                        pbank[bku][:], dbuf[sl][:, 2048 + ci * 128:2048 + (ci + 1) * 128],
                        xn[ci][:, 1 + tik * 512:1 + tik * 512 + 512],
                        start=(ci == 0), stop=(ci == 15)), smm if ci == 15 else None)
                dslot.rel(sl, smm)
                gl = gsl.acq(sched, "s")
                W("s", smm)
                E("s", lambda e, bkg=bkg, ip=ip, gl=gl: e.activation(
                    gsil[gl][:], pbank[bkg][:], AF.Silu, scale=sg0t[:, ip:ip + 1]), ssc)
                gub.rel(bkg, ssc)
                W("v", ssc)
                E("v", lambda e, bku=bku, ip=ip, ipo=ipo, gl=gl: e.scalar_tensor_tensor(
                    fuse[ipo][:], pbank[bku][:], su0t[:, ip:ip + 1], gsil[gl][:],
                    AL.mult, AL.mult), sv)
                gub.rel(bku, sv)
                gsl.rel(gl, sv)
            n_fuse = sv.n
            for co in range(16):
                sl = dslot.acq(sched, "g")
                E("g", lambda e, sl=sl, co=co, ig=ig: e.dma_start(
                    out=dbuf[sl][:],
                    in_=d0[co][:, ig * 32:(ig + 1) * 32, :].rearrange("p a b -> p (a b)")), sw, 16)
                bk = dnb.acq(sched, "t")
                W("t", sw)
                W("t", sv, n_fuse)
                for j in range(32):
                    E("t", lambda e, sl=sl, j=j, bk=bk: e.matmul(
                        pbank[4 + bk][:], dbuf[sl][:, j * 128:(j + 1) * 128], fuse[j][:],
                        start=(j == 0), stop=(j == 31)), smm if j == 31 else None)
                dslot.rel(sl, smm)
                W("v", smm)
                E("v", lambda e, bk=bk, co=co, tik=tik: e.scalar_tensor_tensor(
                    xr[co][:, tik * 512:(tik + 1) * 512], pbank[4 + bk][:],
                    sd0t[:, co:co + 1], xr[co][:, tik * 512:(tik + 1) * 512],
                    AL.mult, AL.add), sv)
                dnb.rel(bk, sv)
            W("t", sv)

    W("g", sv)
    for ct in range(16):
        E("g", lambda e, ct=ct: e.dma_start(
            out=xr[ct][:], in_=x_fm[ct * 128:(ct + 1) * 128, 1:1025],
            accum_op=AL.add), sgd, 16)
    n_x1 = sgd.n

    if stage < 6:
        joinall("g")
        for ct in range(16):
            E("g", lambda e, ct=ct: e.dma_start(out=dbg["dbg_x1"][ct], in_=xr[ct][:]), sout, 16)
    if stage == 3:
        W("y", sout, sout.n)
        E("y", lambda e: e.dma_start(out=out_fm[0:128, :], in_=xr[0][:]), sout, 16)
        W("y", sout, sout.n)
        return finish()

    joinall("t", "v", "s")
    ch2 = [(0, 512), (512, 1024)]
    W("t", sgd, n_x1)
    W("s", sgd, n_x1)
    for ct in range(16):
        W("s", smm)
        E("s", lambda e, ct=ct: e.activation(xsq[:, 0:1024], xr[ct][:], AF.Square), ssc)
        W("t", ssc)
        for ci, (a, b_) in enumerate(ch2):
            E("t", lambda e, a=a, b_=b_, ci=ci, ct=ct: e.matmul(
                pbank[ci][0:1, :], onf[:], xr[ct][:, a:b_],
                start=(ct == 0), stop=(ct == 15)), smm)
            E("t", lambda e, a=a, b_=b_, ci=ci, ct=ct: e.matmul(
                pbank[2 + ci][0:1, :], onf[:], xsq[:, a:b_],
                start=(ct == 0), stop=(ct == 15)), smm)
    emit_ln_lane([0, 1], [2, 3], ch2)
    W("v", ssc)
    for ct in range(16):
        E("v", lambda e, ct=ct: e.tensor_tensor(
            xa[:, 0:1024], xr[ct][:], rstdB[:, 0:1024], AL.mult), sv)
        E("v", lambda e, ct=ct: e.tensor_tensor(
            xn[ct][:, 0:1024], xa[:, 0:1024], mrsB[:, 0:1024], AL.subtract), sv)
    n_xn1 = sv.n

    if stage >= 4 and stage < 6:
        joinall("g")
        for ct in range(16):
            E("g", lambda e, ct=ct: e.dma_start(
                out=dbg["dbg_xn1"][ct], in_=xn[ct][:, 0:1024]), sout, 16)

    joinall("t", "v", "s", "g")
    W("g", sv, n_xn1)
    for ci in range(16):
        E("g", lambda e, ci=ci: e.dma_start(
            out=wv1sb[:, ci * 512:(ci + 1) * 512], in_=wv1[ci]), sw, 16)
    kqb = _Slots(2)
    vsl = _Slots(2)
    ksl = _Slots(2)
    n_wv1 = sw.n
    for tt in range(8):
        bk = kqb.acq(sched, "t")
        W("t", sw, n_wv1)
        W("t", sv, n_xn1)
        for ci in range(16):
            E("t", lambda e, ci=ci, tt=tt, bk=bk: e.matmul(
                pbank[bk][:], xn[ci][:, tt * 128:(tt + 1) * 128],
                wv1sb[:, ci * 512:(ci + 1) * 512],
                start=(ci == 0), stop=(ci == 15)), smm if ci == 15 else None)
        s_ = vsl.acq(sched, "s")
        W("s", smm)
        E("s", lambda e, bk=bk, s_=s_: e.copy(vstg[s_][:], pbank[bk][:]), ssc)
        kqb.rel(bk, ssc)
        W("y", ssc)
        E("y", lambda e, s_=s_, tt=tt: e.dma_start(out=cc2V_in[tt], in_=vstg[s_][:]), sci, 16)
        vsl.rel(s_, sci)
    for kvh in range(4):
        sl = pslot.acq(sched, "g")
        E("g", lambda e, sl=sl, kvh=kvh: e.dma_start(
            out=pbuf[sl][:], in_=wk1[kvh].rearrange("p a b -> p (a b)")), sw, 16)
        for tik in range(2):
            bk = kqb.acq(sched, "t")
            W("t", sw)
            for ci in range(16):
                E("t", lambda e, sl=sl, ci=ci, bk=bk, tik=tik: e.matmul(
                    pbank[bk][:], pbuf[sl][:, ci * 128:(ci + 1) * 128],
                    xn[ci][:, tik * 512:(tik + 1) * 512],
                    start=(ci == 0), stop=(ci == 15)), smm if ci == 15 else None)
            W("s", smm)
            W("s", sv)
            E("s", lambda e, bk=bk: e.copy(t1[:], pbank[bk][:]), ssc)
            kqb.rel(bk, ssc)
            W("t", ssc)
            W("t", sv)
            E("t", lambda e: e.matmul(pbank[7][:], prm[:], t1[:], start=True, stop=True), smm)
            ks = ksl.acq(sched, "v")
            W("v", smm)
            sl0 = tik * 512
            E("v", lambda e, sl0=sl0: e.tensor_tensor(
                t2[:], t1[:], cosT[:, sl0:sl0 + 512], AL.mult), sv)
            E("v", lambda e, sl0=sl0: e.tensor_tensor(
                t1[:], pbank[7][:], sinT[:, sl0:sl0 + 512], AL.mult), sv)
            E("v", lambda e, ks=ks: e.tensor_tensor(
                kstg[ks][:], t2[:], t1[:], AL.add), sv)
            W("y", sv)
            E("y", lambda e, ks=ks, kvh=kvh, tik=tik: e.dma_start(
                out=cc2K_in[kvh][:, tik * 512:(tik + 1) * 512], in_=kstg[ks][:]), sci, 16)
            ksl.rel(ks, sci)
        pslot.rel(sl, smm)
    W("g", sci)
    E("g", lambda e: e.collective_compute(
        "AllGather", AL.bypass, replica_groups=[[0, 1], [2, 3], [4, 5], [6, 7]],
        ins=[cc2_in], outs=[cc2_out]), scc)
    n_cc2 = scc.n

    joinall("t", "v", "s")

    def emit_ffn1(ig, skip_down=False, ntik=2):
        n0 = ig * 22
        for tik in range(ntik):
            for ipo in range(22):
                ip = n0 + ipo
                sl = dslot.acq(sched, "g")
                E("g", lambda e, sl=sl, ip=ip: e.dma_start(
                    out=dbuf[sl][:], in_=gu1[ip].rearrange("p a b -> p (a b)")), sw, 16)
                bkg = gub.acq(sched, "t")
                W("t", sw)
                for ci in range(16):
                    E("t", lambda e, sl=sl, ci=ci, bkg=bkg, tik=tik: e.matmul(
                        pbank[bkg][:], dbuf[sl][:, ci * 128:(ci + 1) * 128],
                        xn[ci][:, tik * 512:(tik + 1) * 512],
                        start=(ci == 0), stop=(ci == 15)), smm if ci == 15 else None)
                bku = gub.acq(sched, "t")
                for ci in range(16):
                    E("t", lambda e, sl=sl, ci=ci, bku=bku, tik=tik: e.matmul(
                        pbank[bku][:], dbuf[sl][:, 2048 + ci * 128:2048 + (ci + 1) * 128],
                        xn[ci][:, tik * 512:(tik + 1) * 512],
                        start=(ci == 0), stop=(ci == 15)), smm if ci == 15 else None)
                dslot.rel(sl, smm)
                gl = gsl.acq(sched, "s")
                W("s", smm)
                E("s", lambda e, bkg=bkg, ip=ip, gl=gl: e.activation(
                    gsil[gl][:], pbank[bkg][:], AF.Silu, scale=gs1t[:, ip:ip + 1]), ssc)
                gub.rel(bkg, ssc)
                W("v", ssc)
                E("v", lambda e, bku=bku, ip=ip, ipo=ipo, gl=gl: e.scalar_tensor_tensor(
                    fuse[ipo][:], pbank[bku][:], us1t[:, ip:ip + 1], gsil[gl][:],
                    AL.mult, AL.mult), sv)
                gub.rel(bku, sv)
                gsl.rel(gl, sv)
            n_fuse = sv.n
            for co in ([] if skip_down else range(16)):
                sl = dslot.acq(sched, "g")
                E("g", lambda e, sl=sl, co=co, ig=ig: e.dma_start(
                    out=dbuf[sl][:, 0:2816],
                    in_=d1[co][:, ig * 22:(ig + 1) * 22, :].rearrange("p a b -> p (a b)")), sw, 16)
                bk = dnb.acq(sched, "t")
                W("t", sw)
                W("t", sv, n_fuse)
                for j in range(22):
                    E("t", lambda e, sl=sl, j=j, bk=bk: e.matmul(
                        pbank[4 + bk][:], dbuf[sl][:, j * 128:(j + 1) * 128], fuse[j][:],
                        start=(j == 0), stop=(j == 21)), smm if j == 21 else None)
                dslot.rel(sl, smm)
                W("v", smm)
                E("v", lambda e, bk=bk, co=co, tik=tik: e.scalar_tensor_tensor(
                    xr[co][:, tik * 512:(tik + 1) * 512], pbank[4 + bk][:],
                    ds1t[:, co:co + 1], xr[co][:, tik * 512:(tik + 1) * 512],
                    AL.mult, AL.add), sv)
                dnb.rel(bk, sv)
            W("t", sv)

    emit_ffn1(0)

    joinall("t", "v", "s", "y")
    W("y", scc, n_cc2)
    ob_free = None
    lb_free = None
    stb = _Slots(2)
    qb_s = _Slots(2)
    pt_s = _Slots(2)
    for kvh in range(4):
        W("y", sv)
        W("y", ssc)
        W("y", smm)
        for r in range(2):
            E("y", lambda e, r=r, kvh=kvh: e.dma_start(
                out=kkv[:, r * 1024:(r + 1) * 1024],
                in_=cc2_out[r, kvh * 131072:(kvh + 1) * 131072].rearrange(
                    "(p n) -> p n", p=128)), sco, 16)
            for tt in range(8):
                E("y", lambda e, r=r, tt=tt, kvh=kvh: e.dma_start(
                    out=vkv[:, (r * 8 + tt) * 128:(r * 8 + tt + 1) * 128],
                    in_=cc2_out[r, 524288 + tt * 65536:524288 + (tt + 1) * 65536].rearrange(
                        "(p n) -> p n", p=128)[:, kvh * 128:(kvh + 1) * 128]), sco, 16)
        n_kv_ld = sco.n
        if stage >= 4 and stage < 6:
            joinall("g")
            E("g", lambda e, kvh=kvh: e.dma_start(out=dbg["dbg_kf"][kvh], in_=kkv[:]), sout, 16)
            if kvh == 0:
                E("g", lambda e: e.dma_start(out=dbg["dbg_vf"], in_=vkv[:]), sout, 16)
        if stage == 4 and kvh == 0:
            W("y", sout, sout.n)
            E("y", lambda e: e.dma_start(out=out_fm[0:128, :], in_=xr[0][:]), sout, 16)
            W("y", sout, sout.n)
            return finish()
        for hh in range(4):
            h = kvh * 4 + hh
            sl = pslot.acq(sched, "g")
            E("g", lambda e, sl=sl, h=h: e.dma_start(
                out=pbuf[sl][:], in_=wq1[h].rearrange("p a b -> p (a b)")), sw, 16)
            qsl = qb_s.acq(sched, "t")
            for tik in range(2):
                bk = kqb.acq(sched, "t")
                W("t", sw)
                W("t", sv, n_xn1)
                for ci in range(16):
                    E("t", lambda e, sl=sl, ci=ci, bk=bk, tik=tik: e.matmul(
                        pbank[bk][:], pbuf[sl][:, ci * 128:(ci + 1) * 128],
                        xn[ci][:, tik * 512:(tik + 1) * 512],
                        start=(ci == 0), stop=(ci == 15)), smm if ci == 15 else None)
                W("s", smm)
                W("s", sv)
                E("s", lambda e, bk=bk: e.copy(t1[:], pbank[bk][:]), ssc)
                kqb.rel(bk, ssc)
                W("t", ssc)
                W("t", sv)
                E("t", lambda e: e.matmul(pbank[7][:], prm[:], t1[:], start=True, stop=True), smm)
                W("v", smm)
                sl0 = tik * 512
                E("v", lambda e, sl0=sl0: e.tensor_tensor(
                    t2[:], t1[:], cosT[:, sl0:sl0 + 512], AL.mult), sv)
                E("v", lambda e, sl0=sl0: e.tensor_tensor(
                    t1[:], pbank[7][:], sinT[:, sl0:sl0 + 512], AL.mult), sv)
                E("v", lambda e, qsl=qsl, sl0=sl0: e.tensor_tensor(
                    qbf[qsl][:, sl0:sl0 + 512], t2[:], t1[:], AL.add), sv)
            pslot.rel(sl, smm)
            n_q = sv.n
            for qt in range(2):
                band = set(range(4 * qt, 4 * qt + 4)) | set(range(8 + 4 * qt, 8 + 4 * qt + 4))
                exp_done = {}
                mask_done = {}

                def emit_st(kt):
                    bk2 = stb.acq(sched, "t")
                    W("t", sco, n_kv_ld)
                    W("t", sv, n_q)
                    E("t", lambda e, kt=kt, bk2=bk2, qt=qt, qsl=qsl: e.matmul(
                        pbank[2 + bk2][:], kkv[:, kt * 128:(kt + 1) * 128],
                        qbf[qsl][:, qt * 512:(qt + 1) * 512],
                        start=True, stop=True), smm)
                    return bk2

                bks = {0: emit_st(0)}
                for kt in range(16):
                    if kt + 1 < 16:
                        bks[kt + 1] = emit_st(kt + 1)
                    pts = pt_s.acq(sched, "s")
                    W("s", smm)
                    bidx = qt * 16 + kt
                    bkv = bks[kt]
                    E("s", lambda e, bkv=bkv, pts=pts, bidx=bidx: e.activation(
                        ptb[pts][:], pbank[2 + bkv][:], AF.Exp,
                        bias=btbl[:, bidx:bidx + 1], scale=ROPE_SCALE), ssc)
                    stb.rel(bkv, ssc)
                    exp_done[kt] = ssc.n
                    if kt in band:
                        slot = (0 if kt < 8 else 4) + (kt % 4)
                        W("v", ssc, exp_done[kt])
                        E("v", lambda e, pts=pts, slot=slot: e.tensor_tensor(
                            ptb[pts][:], ptb[pts][:], bmsk[:, slot * 512:(slot + 1) * 512],
                            AL.mult), sv)
                        mask_done[kt] = sv.n
                        W("t", sv, mask_done[kt])
                    else:
                        W("t", ssc, exp_done[kt])
                    if kt == 0:
                        if ob_free is not None:
                            W("t", ob_free[0], ob_free[1])
                        if lb_free is not None:
                            W("t", lb_free[0], lb_free[1])
                    E("t", lambda e, pts=pts, kt=kt: e.matmul(
                        pbank[4][:], vkv[:, kt * 128:(kt + 1) * 128], ptb[pts][:],
                        start=(kt == 0), stop=(kt == 15)), smm if kt == 15 else None)
                    E("t", lambda e, pts=pts, kt=kt: e.matmul(
                        pbank[5][0:1, :], onbf[:], ptb[pts][:],
                        start=(kt == 0), stop=(kt == 15)), smm if kt == 15 else None)
                    pt_s.rel(pts, smm)
                W("s", smm)
                E("s", lambda e: e.activation(lscr[0:1, :], pbank[5][0:1, :], AF.Ln), ssc)
                E("s", lambda e: e.activation(
                    lrec[0:1, :], lscr[0:1, :], AF.Exp, scale=-1.0), ssc)
                lb_free = (ssc, ssc.n)
                W("t", sv)
                E("t", lambda e: e.matmul(pbank[6][:], onrow[:], lrec[0:1, :],
                                          start=True, stop=True), smm)
                W("s", smm)
                E("s", lambda e: e.copy(lrecB[:], pbank[6][:]), ssc)
                W("v", ssc)
                E("v", lambda e, h=h, qt=qt: e.tensor_tensor(
                    ai[h][:, qt * 512:(qt + 1) * 512], pbank[4][:], lrecB[:], AL.mult), sv)
                ob_free = (sv, sv.n)
            qb_s.rel(qsl, smm)
    n_attn = sv.n

    if stage >= 5 and stage < 6:
        joinall("g")
        for ct in range(16):
            E("g", lambda e, ct=ct: e.dma_start(out=dbg["dbg_on"][ct], in_=ai[ct][:]), sout, 16)
    if stage == 56:
        pass
    if stage == 5:
        W("y", sout, sout.n)
        E("y", lambda e: e.dma_start(out=out_fm[0:128, :], in_=xr[0][:]), sout, 16)
        W("y", sout, sout.n)
        return finish()

    joinall("t", "v", "s")
    for co in ([] if stage == 56 else range(16)):
        sl = pslot.acq(sched, "g")
        E("g", lambda e, sl=sl, co=co: e.dma_start(
            out=pbuf[sl][:], in_=wo1[co].rearrange("p a b -> p (a b)")), sw, 16)
        W("t", sw)
        W("t", sv, n_attn)
        for tik in range(2):
            bk = wob.acq(sched, "t")
            for ht in range(16):
                E("t", lambda e, sl=sl, ht=ht, bk=bk, tik=tik: e.matmul(
                    pbank[bk][:], pbuf[sl][:, ht * 128:(ht + 1) * 128],
                    ai[ht][:, tik * 512:(tik + 1) * 512],
                    start=(ht == 0), stop=(ht == 15)), smm if ht == 15 else None)
            W("v", smm)
            E("v", lambda e, bk=bk, co=co, tik=tik: e.tensor_tensor(
                xr[co][:, tik * 512:(tik + 1) * 512],
                xr[co][:, tik * 512:(tik + 1) * 512], pbank[bk][:], AL.add), sv)
            wob.rel(bk, sv)
        pslot.rel(sl, smm)

    if stage != 55:
        joinall("t", "v", "s", "g")
        if stage == 57:
            emit_ffn1(1, skip_down=True)
        elif stage == 58:
            emit_ffn1(1, ntik=1)
        else:
            emit_ffn1(1)

    joinall("y")
    for ct in range(16):
        E("y", lambda e, ct=ct: e.dma_start(
            out=out_fm[ct * 128:(ct + 1) * 128, :], in_=xr[ct][:]), sout, 16)
    W("y", sout, sout.n)
    return finish()


def _build_and_run(inputs, stage=6):
    maps = _host_inputs(inputs)
    nc, es, dbg = build_nc(stage)
    try:
        res = run_bass_kernel_spmd(nc, maps, core_ids=list(range(8)))
    finally:
        es.close()
    return res


def kernel(**inputs):
    res = _build_and_run(inputs, stage=6)
    out = np.empty((B, T, H), np.float32)
    for c in range(8):
        b, s = c // 2, c % 2
        out[b, s * HALF:(s + 1) * HALF, :] = np.asarray(res.results[c]["out_fm"]).T
    return out


# version 03
# speedup vs baseline: 1.1384x; 1.1384x over previous
import contextlib

import numpy as np
import ml_dtypes

import concourse.bass as bass
import concourse.mybir as mybir
from concourse.bass_utils import run_bass_kernel_spmd
from concourse.alu_op_type import AluOpType as AL

F32 = mybir.dt.float32
BF16 = mybir.dt.bfloat16
F16 = mybir.dt.float16
I8 = mybir.dt.int8
AF = mybir.ActivationFunctionType
BFD = ml_dtypes.bfloat16

B, T, H = 4, 2048, 2048
HALF = 1024
NI0, NI1 = 64, 44
ROPE_SCALE = 128 ** -0.5


def _tile_w(W, nct, ncp):
    return np.ascontiguousarray(
        np.asarray(W).reshape(nct, 128, ncp, 128).transpose(2, 1, 0, 3))


def _vec_pack(v, n):
    return np.ascontiguousarray(np.asarray(v, np.float32).reshape(n, 128).T)


def _host_inputs(I):
    x = np.asarray(I["L0_attn_hidden"], np.float32)
    sh = {}
    sh["wk0"] = _tile_w(I["L0_attn_Wk"], 16, 16).astype(np.float32)
    sh["wv0"] = _tile_w(I["L0_attn_Wv"], 16, 16).astype(np.float32)
    sh["wr0"] = _tile_w(I["L0_attn_Wr"], 16, 16).astype(np.float32)
    sh["wo0"] = _tile_w(I["L0_attn_Wo"], 16, 16).astype(np.float32)
    g0 = _tile_w(I["L0_ffn_w_gate"], 16, NI0)
    u0 = _tile_w(I["L0_ffn_w_up"], 16, NI0)
    sh["gu0"] = np.ascontiguousarray(np.concatenate([g0, u0], axis=2)).astype(np.int8)
    sh["d0"] = np.ascontiguousarray(_tile_w(I["L0_ffn_w_down"], NI0, 16)).astype(np.int8)
    sh["wq1"] = _tile_w(I["L1_attn_Wq"], 16, 16).astype(np.float32)
    sh["wk1"] = _tile_w(I["L1_attn_Wk"], 16, 4).astype(np.float32)
    sh["wv1"] = np.ascontiguousarray(
        np.asarray(I["L1_attn_Wv"], np.float32).reshape(16, 128, 512))
    sh["wo1"] = _tile_w(I["L1_attn_Wo"], 16, 16).astype(np.float32)
    g1 = _tile_w(np.asarray(I["L1_ffn_gate_w"], np.float32).T, 16, NI1)
    u1 = _tile_w(np.asarray(I["L1_ffn_up_w"], np.float32).T, 16, NI1)
    sh["gu1"] = np.ascontiguousarray(np.concatenate([g1, u1], axis=2)).astype(np.float32)
    sh["d1"] = np.ascontiguousarray(
        _tile_w(np.asarray(I["L1_ffn_down_w"], np.float32).T, NI1, 16)).astype(np.float32)

    vp = np.stack([
        _vec_pack(I["L0_attn_time_decay"], 16),
        _vec_pack(I["L0_attn_time_first"], 16),
        _vec_pack(np.asarray(I["L0_attn_time_mix_key"]).reshape(-1), 16),
        _vec_pack(np.asarray(I["L0_attn_time_mix_value"]).reshape(-1), 16),
        _vec_pack(np.asarray(I["L0_attn_time_mix_receptance"]).reshape(-1), 16),
    ], axis=1)
    sh["vpk"] = np.ascontiguousarray(vp.reshape(128, 80))
    sh["sg0"] = _vec_pack(I["L0_ffn_s_gate"], NI0)
    sh["su0"] = _vec_pack(I["L0_ffn_s_up"], NI0)
    sh["sd0"] = _vec_pack(I["L0_ffn_s_down"], 16)
    sh["gs1"] = _vec_pack(I["L1_ffn_gate_s"], NI1)
    sh["us1"] = _vec_pack(I["L1_ffn_up_s"], NI1)
    sh["ds1"] = _vec_pack(I["L1_ffn_down_s"], 16)
    sh["onesf"] = np.ones((128, 1), np.float32)
    sh["onesrow"] = np.ones((1, 128), np.float32)
    sh["onesbf"] = np.ones((128, 1), BFD)
    sh["perm"] = np.ascontiguousarray(np.roll(np.eye(128, dtype=np.float32), 64, axis=0))

    t_ = np.arange(T, dtype=np.float32)
    inv_freq = (1.0 / 10000.0 ** (np.arange(0, 128, 2, dtype=np.float32) / 128)).astype(np.float32)
    freqs = t_[:, None] * inv_freq[None, :]
    cos_t = np.cos(freqs).astype(np.float32)
    sin_t = np.sin(freqs).astype(np.float32)
    tri = (np.arange(512)[:, None] <= np.arange(512)[None, :]).astype(np.float32)

    maps = []
    for c in range(8):
        b, s = c // 2, c % 2
        t0 = s * HALF
        m = dict(sh)
        xo = np.zeros((HALF + 1, H), np.float32)
        if t0 > 0:
            xo[0] = x[b, t0 - 1]
        xo[1:] = x[b, t0:t0 + HALF]
        m["x_fm"] = np.ascontiguousarray(xo.T)
        m["smask"] = np.full((128, 1), float(s), np.float32)
        cs = cos_t[t0:t0 + HALF].T
        sn = sin_t[t0:t0 + HALF].T
        m["cosq"] = np.ascontiguousarray(np.vstack([cs, cs]))
        m["sinq"] = np.ascontiguousarray(np.vstack([-sn, sn]))
        bm = np.zeros((8, 128, 512), np.float32)
        for j in range(4):
            bm[j] = tri[j * 128:(j + 1) * 128, :] if s == 0 else 1.0
            bm[4 + j] = 0.0 if s == 0 else tri[j * 128:(j + 1) * 128, :]
        m["bandmask"] = np.ascontiguousarray(
            bm.transpose(1, 0, 2).reshape(128, 8 * 512)).astype(BFD)
        bt = np.zeros((2, 16), np.float32)
        for qt in range(2):
            band = {4 * qt + j for j in range(4)} | {8 + 4 * qt + j for j in range(4)}
            for kt in range(16):
                if kt in band or kt < 8 * s + 4 * qt:
                    bt[qt, kt] = 0.0
                else:
                    bt[qt, kt] = -1e9
        m["biastbl"] = np.ascontiguousarray(
            np.broadcast_to(bt.reshape(1, 32), (128, 32)).astype(np.float32))
        maps.append(m)
    return maps


class _Sem:
    __slots__ = ("h", "n")

    def __init__(self, es, nc, name):
        self.h = es.enter_context(nc.semaphore(name))
        self.n = 0


class _Slots:
    def __init__(self, n):
        self.gate = [None] * n
        self.i = 0

    def acq(self, sched, eng):
        s = self.i % len(self.gate)
        self.i += 1
        g = self.gate[s]
        if g is not None:
            _w(sched, eng, g[0], g[1])
        return s

    def rel(self, s, sem):
        self.gate[s] = (sem, sem.n)


def _w(sched, eng, sem, val=None):
    v = sem.n if val is None else val
    if v > 0:
        sched[eng].append(lambda e, h=sem.h, v=v: e.wait_ge(h, v))


def _op(sched, eng, fn, sem=None, k=1):
    if sem is not None:
        sem.n += k
        sched[eng].append(lambda e, fn=fn, h=sem.h, k=k: fn(e).then_inc(h, k))
    else:
        sched[eng].append(lambda e, fn=fn: fn(e))


def build_nc(stage=6):
    nocc = stage == 21
    if nocc:
        stage = 2
    nc = bass.Bass("TRN2", target_bir_lowering=False, debug=False)
    es = contextlib.ExitStack()

    def din(name, shape, dt):
        return nc.dram_tensor(name, shape, dt, kind="ExternalInput").ap()

    x_fm = din("x_fm", [2048, 1025], F32)
    wk0 = din("wk0", [16, 128, 16, 128], F32)
    wv0 = din("wv0", [16, 128, 16, 128], F32)
    wr0 = din("wr0", [16, 128, 16, 128], F32)
    wo0 = din("wo0", [16, 128, 16, 128], F32)
    gu0 = din("gu0", [NI0, 128, 32, 128], I8)
    d0 = din("d0", [16, 128, NI0, 128], I8)
    wq1 = din("wq1", [16, 128, 16, 128], F32)
    wk1 = din("wk1", [4, 128, 16, 128], F32)
    wv1 = din("wv1", [16, 128, 512], F32)
    wo1 = din("wo1", [16, 128, 16, 128], F32)
    gu1 = din("gu1", [NI1, 128, 32, 128], F32)
    d1 = din("d1", [16, 128, NI1, 128], F32)
    vpk = din("vpk", [128, 80], F32)
    sg0 = din("sg0", [128, NI0], F32)
    su0 = din("su0", [128, NI0], F32)
    sd0 = din("sd0", [128, 16], F32)
    gs1 = din("gs1", [128, NI1], F32)
    us1 = din("us1", [128, NI1], F32)
    ds1 = din("ds1", [128, 16], F32)
    onesf = din("onesf", [128, 1], F32)
    onesrow = din("onesrow", [1, 128], F32)
    onesbf = din("onesbf", [128, 1], BF16)
    permi = din("perm", [128, 128], F32)
    smask = din("smask", [128, 1], F32)
    cosq = din("cosq", [128, 1024], F32)
    sinq = din("sinq", [128, 1024], F32)
    bandmask = din("bandmask", [128, 8 * 512], BF16)
    biastbl = din("biastbl", [128, 32], F32)

    out_fm = nc.dram_tensor("out_fm", [2048, 1024], F32, kind="ExternalOutput").ap()

    cc1_in = nc.dram_tensor("cc1_in", [128, 32], F32, kind="Internal").ap()
    cc1_out = nc.dram_tensor("cc1_out", [2, 128, 32], F32, kind="Internal").ap()
    cc2_in = nc.dram_tensor("cc2_in", [1048576], BF16, kind="Internal").ap()
    cc2_out = nc.dram_tensor("cc2_out", [2, 1048576], BF16, kind="Internal").ap()
    cc2K_in = cc2_in[0:524288].rearrange("(h p n) -> h p n", h=4, p=128)
    cc2V_in = cc2_in[524288:1048576].rearrange("(t p n) -> t p n", t=8, p=128)

    dbg = {}
    if stage < 6:
        def dbgt(name, shape):
            dbg[name] = nc.dram_tensor(name, shape, F32, kind="ExternalOutput").ap()
        dbgt("dbg_xn", [16, 128, 1025])
        dbgt("dbg_k", [16, 128, 1024])
        dbgt("dbg_v", [16, 128, 1024])
        if stage >= 2:
            dbgt("dbg_ai", [16, 128, 1024])
            dbgt("dbg_state", [128, 32])
        if stage >= 3:
            dbgt("dbg_x1", [16, 128, 1024])
        if stage >= 4:
            dbgt("dbg_xn1", [16, 128, 1024])
            dbgt("dbg_kf", [4, 128, 2048])
            dbgt("dbg_vf", [128, 2048])
        if stage >= 5:
            dbgt("dbg_on", [16, 128, 1024])

    ARENA = 207136
    _ar = nc.alloc_sbuf_tensor("arena", [128, ARENA], I8)
    _base = nc.lookup_mloc(_ar).addr

    def sb(name, shape, dt, off):
        return nc.alloc_sbuf_tensor_at(name, shape, dt, offset=_base + off).ap()

    XN, XR, AI, SP, SCb = 0, 33280, 98816, 132096, 152896
    xn = [sb(f"xn{i}", [128, 1025], BF16, XN + i * 2080) for i in range(16)]
    xr = [sb(f"xr{i}", [128, 1024], F32, XR + i * 4096) for i in range(16)]
    kcar = [sb(f"kc{i}", [128, 1024], F16, XR + i * 4096) for i in range(16)]
    vcar = [sb(f"vc{i}", [128, 1024], F16, XR + i * 4096 + 2048) for i in range(16)]
    ai = [sb(f"ai{i}", [128, 1024], BF16, AI + i * 2080) for i in range(16)]
    fuse = [sb(f"fu{i}", [128, 512], BF16, AI + i * 1024) for i in range(32)]
    xa = sb("xa", [128, 1025], F32, SP + 0)
    xb = sb("xb", [128, 1025], F32, SP + 4160)
    xsq = sb("xsq", [128, 1025], F32, SP + 8320)
    rstdB = sb("rstdB", [128, 1025], F32, SP + 12480)
    mrsB = sb("mrsB", [128, 1025], F32, SP + 16640)
    mixp = [sb(f"mx{i}", [128, 512], BF16, SP + i * 1024) for i in range(16)]
    atile = sb("atile", [128, 512], F32, SP + 16704)
    wt = sb("wt", [128, 1025], F32, SP + 0)
    qs = sb("qs", [128, 1025], F32, SP + 4160)
    psc = sb("psc", [128, 1025], F32, SP + 8320)
    NUMt = sb("NUMt", [128, 1025], F32, SP + 12480)
    DENt = sb("DENt", [128, 1025], F32, SP + 16640)
    wv1sb = sb("wv1sb", [128, 16 * 512], BF16, SP + 0)
    vstg = [sb(f"vst{i}", [128, 512], BF16, SP + 16384 + i * 1024) for i in range(2)]
    kstg = [sb(f"kst{i}", [128, 512], BF16, SP + 18432 + i * 1024) for i in range(2)]

    o = SCb
    def nx(sz):
        nonlocal o
        r = o
        o += sz
        return r
    tdp = sb("tdp", [128, 16], F32, nx(64))
    tfp = sb("tfp", [128, 16], F32, nx(64))
    tmkp = sb("tmkp", [128, 16], F32, nx(64))
    tmvp = sb("tmvp", [128, 16], F32, nx(64))
    tmrp = sb("tmrp", [128, 16], F32, nx(64))
    etfp = sb("etfp", [128, 16], F32, nx(64))
    smk = sb("smk", [128, 1], F32, nx(32))
    sg0t = sb("sg0t", [128, NI0], F32, nx(256))
    su0t = sb("su0t", [128, NI0], F32, nx(256))
    sd0t = sb("sd0t", [128, 16], F32, nx(64))
    gs1t = sb("gs1t", [128, NI1], F32, nx(192))
    us1t = sb("us1t", [128, NI1], F32, nx(192))
    ds1t = sb("ds1t", [128, 16], F32, nx(64))
    onf = sb("onf", [128, 1], F32, nx(32))
    onrow = sb("onrow", [1, 128], F32, nx(512))
    onbf = sb("onbf", [128, 1], BF16, nx(32))
    prm = sb("prm", [128, 128], F32, nx(512))
    btbl = sb("btbl", [128, 32], F32, nx(128))
    bmsk = sb("bmsk", [128, 8 * 512], BF16, nx(8192))
    cosT = sb("cosT", [128, 1024], F32, nx(4096))
    sinT = sb("sinT", [128, 1024], F32, nx(4096))
    stpk = sb("stpk", [128, 32], F32, nx(128))
    rcvs = sb("rcvs", [128, 32], F32, nx(128))
    mrcv = sb("mrcv", [128, 32], F32, nx(128))
    lrec = sb("lrec", [1, 512], F32, nx(2048))
    lscr = sb("lscr", [1, 512], F32, nx(2048))
    t1 = sb("t1", [128, 512], F32, nx(2048))
    t2 = sb("t2", [128, 512], F32, nx(2048))
    st1 = sb("st1", [1, 1025], F32, AI + 0)
    st2 = sb("st2", [1, 1025], F32, AI + 4160)
    st3 = sb("st3", [1, 1025], F32, AI + 8320)
    wtC = sb("wtC", [128, 1025], F32, AI + 0)
    qsC = sb("qsC", [128, 1025], F32, AI + 4160)
    pscC = sb("pscC", [128, 1025], F32, AI + 8320)
    NUMc = sb("NUMc", [128, 1025], F32, AI + 12480)
    DENc = sb("DENc", [128, 1025], F32, AI + 16640)
    gsil = [sb(f"gsil{i}", [128, 512], BF16, nx(1024)) for i in range(2)]
    kkv = sb("kkv", [128, 2048], BF16, SP + 0)
    vkv = sb("vkv", [128, 2048], BF16, SP + 4096)
    qbf = [sb(f"qbf{i}", [128, 1024], BF16, SP + 8192 + i * 2048) for i in range(2)]
    ptb = [sb(f"ptb{i}", [128, 512], BF16, SP + 12288 + i * 1024) for i in range(2)]
    lrecB = sb("lrecB", [128, 512], F32, SP + 14336)
    ptb.append(sb("ptb2x", [128, 512], BF16, SP + 16384))
    lrec2 = sb("lrec2", [1, 512], F32, SP + 17408)
    pbuf = [sb(f"pbW{i}", [128, 2048], BF16, nx(4096)) for i in range(2)]
    dbuf = [sb(f"dbW{i}", [128, 4096], BF16, nx(8192)) for i in range(2)]
    assert o <= ARENA, o

    pbank = [es.enter_context(nc.psum_tensor(f"ps{i}", [128, 512], F32))
             for i in range(8)]

    sched = {"t": [], "v": [], "s": [], "g": [], "y": []}
    sx = _Sem(es, nc, "sx")
    sw = _Sem(es, nc, "sw")
    smm = _Sem(es, nc, "smm")
    sv = _Sem(es, nc, "sv")
    ssc = _Sem(es, nc, "ssc")
    sgd = _Sem(es, nc, "sgd")
    scc = _Sem(es, nc, "scc")
    sci = _Sem(es, nc, "sci")
    sco = _Sem(es, nc, "sco")
    sout = _Sem(es, nc, "sout")

    def W(eng, sem, val=None):
        _w(sched, eng, sem, val)

    def E(eng, fn, sem=None, k=1):
        _op(sched, eng, fn, sem, k)

    def joinall(*engs):
        for eng in engs:
            for s_ in (smm, sv, ssc, sx, sw, sgd, sci, sco, scc):
                W(eng, s_)

    def finish():
        with nc.Block() as blk:
            @blk.sync
            def _(e):
                for f in sched["y"]:
                    f(e)

            @blk.gpsimd
            def _(e):
                for f in sched["g"]:
                    f(e)

            @blk.tensor
            def _(e):
                for f in sched["t"]:
                    f(e)

            @blk.vector
            def _(e):
                for f in sched["v"]:
                    f(e)

            @blk.scalar
            def _(e):
                for f in sched["s"]:
                    f(e)
        return nc, es, dbg

    for dst, src in [(tdp, vpk[:, 0:16]), (tfp, vpk[:, 16:32]), (tmkp, vpk[:, 32:48]),
                     (tmvp, vpk[:, 48:64]), (tmrp, vpk[:, 64:80]),
                     (smk, smask), (sg0t, sg0), (su0t, su0), (sd0t, sd0),
                     (gs1t, gs1), (us1t, us1), (ds1t, ds1), (onf, onesf),
                     (onrow, onesrow), (onbf, onesbf), (prm, permi), (btbl, biastbl),
                     (bmsk, bandmask), (cosT, cosq), (sinT, sinq)]:
        E("y", lambda e, d=dst, s=src: e.dma_start(out=d[:], in_=s), sx, 16)
    n_const = sx.n
    W("s", sx, n_const)
    E("s", lambda e: e.activation(tdp[:], tdp[:], AF.Exp), ssc)
    E("s", lambda e: e.activation(etfp[:], tfp[:], AF.Exp), ssc)
    W("v", ssc)
    E("v", lambda e: e.tensor_scalar_mul(tdp[:], tdp[:], -1.0), sv)
    n_td = sv.n

    chunks = [(0, 512), (512, 1024), (1024, 1025)]
    xld = _Slots(2)
    xbufs = [xa, xb]
    for ct in range(16):
        sl = xld.acq(sched, "y")
        E("y", lambda e, d=xbufs[sl], ct=ct: e.dma_start(
            out=d[:], in_=x_fm[ct * 128:(ct + 1) * 128, :]), sx, 16)
        W("s", sx)
        W("s", smm)
        E("s", lambda e, s_=xbufs[sl]: e.activation(xsq[:], s_[:], AF.Square), ssc)
        W("t", sx)
        for ci, (a, b_) in enumerate(chunks):
            E("t", lambda e, s_=xbufs[sl], a=a, b_=b_, ci=ci, ct=ct: e.matmul(
                pbank[ci][0:1, 0:b_ - a], onf[:], s_[:, a:b_],
                start=(ct == 0), stop=(ct == 15)), smm if ci == 2 else None)
        W("t", ssc)
        for ci, (a, b_) in enumerate(chunks):
            E("t", lambda e, a=a, b_=b_, ci=ci, ct=ct: e.matmul(
                pbank[3 + ci][0:1, 0:b_ - a], onf[:], xsq[:, a:b_],
                start=(ct == 0), stop=(ct == 15)), smm if ci == 2 else None)
        xld.rel(sl, smm)

    def emit_ln_lane(pb_sum, pb_sq, chs):
        n = chs[-1][1]
        W("v", smm)
        for idx, (a, b_) in enumerate(chs):
            w_ = b_ - a
            E("v", lambda e, a=a, w_=w_, i=pb_sum[idx]: e.tensor_scalar(
                st1[0:1, a:a + w_], pbank[i][0:1, 0:w_], 1.0 / 2048.0, None, AL.mult), sv)
            E("v", lambda e, a=a, w_=w_, i=pb_sq[idx]: e.tensor_scalar(
                st2[0:1, a:a + w_], pbank[i][0:1, 0:w_], 1.0 / 2048.0, None, AL.mult), sv)
        E("v", lambda e, n=n: e.tensor_tensor(st3[0:1, 0:n], st1[0:1, 0:n], st1[0:1, 0:n], AL.mult), sv)
        E("v", lambda e, n=n: e.tensor_tensor(st3[0:1, 0:n], st2[0:1, 0:n], st3[0:1, 0:n], AL.subtract), sv)
        E("v", lambda e, n=n: e.tensor_scalar(st3[0:1, 0:n], st3[0:1, 0:n], 1e-5, None, AL.add), sv)
        W("s", sv)
        E("s", lambda e, n=n: e.activation(st2[0:1, 0:n], st3[0:1, 0:n], AF.Ln), ssc)
        E("s", lambda e, n=n: e.activation(st3[0:1, 0:n], st2[0:1, 0:n], AF.Exp,
                                           scale=-0.5), ssc)
        W("v", ssc)
        E("v", lambda e, n=n: e.tensor_tensor(
            st1[0:1, 0:n], st1[0:1, 0:n], st3[0:1, 0:n], AL.mult), sv)
        W("t", sv)
        for dst, row in [(rstdB, st3), (mrsB, st1)]:
            for (a, b_) in chs:
                E("t", lambda e, a=a, b_=b_, row=row: e.matmul(
                    pbank[6][:, 0:b_ - a], onrow[:], row[0:1, a:b_],
                    start=True, stop=True), smm)
                W("s", smm)
                E("s", lambda e, a=a, b_=b_, dst=dst: e.copy(
                    dst[:, a:b_], pbank[6][:, 0:b_ - a]), ssc)
                W("t", ssc)

    emit_ln_lane([0, 1, 2], [3, 4, 5], chunks)
    for ct in range(16):
        sl = xld.acq(sched, "y")
        E("y", lambda e, d=xbufs[sl], ct=ct: e.dma_start(
            out=d[:], in_=x_fm[ct * 128:(ct + 1) * 128, :]), sx, 16)
        W("v", sx)
        W("v", ssc)
        E("v", lambda e, s_=xbufs[sl]: e.tensor_tensor(s_[:], s_[:], rstdB[:], AL.mult), sv)
        E("v", lambda e, s_=xbufs[sl], ct=ct: e.tensor_tensor(
            xn[ct][:], s_[:], mrsB[:], AL.subtract), sv)
        xld.rel(sl, sv)
    n_xn = sv.n

    pslot = _Slots(2)
    kvb = _Slots(4)
    W("v", sx, n_const)
    E("v", lambda e: e.memset(wtC[:, 0:1], 1.0), sv)
    E("v", lambda e: e.memset(qsC[:, 0:1], 0.0), sv)
    E("v", lambda e: e.memset(pscC[:, 0:1], 0.0), sv)
    n_scans_done = None
    for wsrc, tmx, kind in [(wk0, tmkp, "k"), (wv0, tmvp, "v"), (wr0, tmrp, "r")]:
        for tik in range(2):
            cur0, sh0 = 1 + tik * 512, tik * 512
            W("v", smm)
            for ci in range(16):
                E("v", lambda e, ci=ci, cur0=cur0, sh0=sh0: e.tensor_tensor(
                    atile[:], xn[ci][:, cur0:cur0 + 512], xn[ci][:, sh0:sh0 + 512],
                    AL.subtract), sv)
                E("v", lambda e, ci=ci, sh0=sh0, tmx=tmx: e.scalar_tensor_tensor(
                    mixp[ci][:], atile[:], tmx[:, ci:ci + 1], xn[ci][:, sh0:sh0 + 512],
                    AL.mult, AL.add), sv)
            n_mix = sv.n
            if kind == "r" and tik == 0:
                n_r_gate = sv.n
            if kind == "r" and tik == 1 and not nocc:
                W("g", sci)
                E("g", lambda e: (
                    e.collective_compute(
                        "AllGather", AL.bypass,
                        replica_groups=[[0, 1], [2, 3], [4, 5], [6, 7]],
                        ins=[cc1_in], outs=[cc1_out])), scc)
            for co in range(16):
                sl = pslot.acq(sched, "g")
                E("g", lambda e, sl=sl, w_=wsrc, co=co: e.dma_start(
                    out=pbuf[sl][:], in_=w_[co].rearrange("p a b -> p (a b)")), sw, 16)
                bk = kvb.acq(sched, "t")
                W("t", sw)
                W("t", sv, n_mix)
                for ci in range(16):
                    E("t", lambda e, sl=sl, ci=ci, bk=bk: e.matmul(
                        pbank[bk][:], pbuf[sl][:, ci * 128:(ci + 1) * 128], mixp[ci][:],
                        start=(ci == 0), stop=(ci == 15)), smm if ci == 15 else None)
                pslot.rel(sl, smm)
                W("s", smm)
                if kind == "r":
                    if co == 0 and tik == 0:
                        W("s", sv, n_r_gate)
                    E("s", lambda e, bk=bk, co=co, tik=tik: e.activation(
                        ai[co][:, tik * 512:(tik + 1) * 512], pbank[bk][:],
                        AF.Sigmoid), ssc)
                else:
                    dcar = kcar if kind == "k" else vcar
                    E("s", lambda e, bk=bk, co=co, tik=tik, d=dcar: e.copy(
                        d[co][:, tik * 512:(tik + 1) * 512], pbank[bk][:]), ssc)
                kvb.rel(bk, ssc)
                if kind == "v" and tik == 1:
                    ct = co
                    W("s", sv)
                    W("s", sv, n_td)
                    E("s", lambda e, ct=ct: e.activation(
                        qsC[:, 1:1025], kcar[ct][:], AF.Exp), ssc)
                    E("s", lambda e, ct=ct: e.activation(
                        wtC[:, 1:1025], xn[0][:, 0:1024], AF.Exp,
                        bias=tdp[:, ct:ct + 1], scale=0.0), ssc)
                    W("v", ssc)
                    E("v", lambda e, ct=ct: e.tensor_tensor(
                        pscC[:, 1:1025], qsC[:, 1:1025], vcar[ct][:], AL.mult), sv)
                    E("v", lambda e: e.tensor_tensor_scan(
                        NUMc[:], wtC[:], pscC[:], 0.0, AL.mult, AL.add), sv)
                    E("v", lambda e: e.tensor_tensor_scan(
                        DENc[:], wtC[:], qsC[:], 0.0, AL.mult, AL.add), sv)
                    E("v", lambda e, ct=ct: e.tensor_copy(
                        stpk[:, ct:ct + 1], NUMc[:, 1024:1025]), sv)
                    E("v", lambda e, ct=ct: e.tensor_copy(
                        stpk[:, 16 + ct:17 + ct], DENc[:, 1024:1025]), sv)
            if kind == "v" and tik == 1:
                W("y", sv)
                E("y", lambda e: e.dma_start(out=cc1_in, in_=stpk[:]), sci, 16)
    n_kv = ssc.n
    n_r = ssc.n

    if stage < 6:
        joinall("g")
        for ct in range(16):
            E("g", lambda e, ct=ct: e.dma_start(out=dbg["dbg_xn"][ct], in_=xn[ct][:]), sout, 16)
            E("g", lambda e, ct=ct: e.dma_start(out=dbg["dbg_k"][ct], in_=kcar[ct][:]), sout, 16)
            E("g", lambda e, ct=ct: e.dma_start(out=dbg["dbg_v"][ct], in_=vcar[ct][:]), sout, 16)
    if stage == 1:
        W("y", sout, sout.n)
        E("y", lambda e: e.dma_start(out=out_fm[0:128, :], in_=xr[0][:]), sout, 16)
        W("y", sout, sout.n)
        return finish()

    if nocc:
        E("v", lambda e: e.memset(mrcv[:], 0.0), sv)
    else:
        W("y", scc)
        E("y", lambda e: e.dma_start(out=rcvs[:], in_=cc1_out[0]), sco, 16)
        W("v", sco)
        E("v", lambda e: e.tensor_scalar_mul(mrcv[:], rcvs[:], smk[:, 0:1]), sv)

    W("v", smm)
    E("v", lambda e: e.memset(wt[:, 0:1], 1.0), sv)
    E("v", lambda e: e.memset(qs[:, 0:1], 0.0), sv)
    E("v", lambda e: e.memset(psc[:, 0:1], 0.0), sv)
    W("v", ssc, n_r)
    for ct in range(16):
        W("s", sv)
        E("s", lambda e, ct=ct: e.activation(qs[:, 1:1025], kcar[ct][:], AF.Exp), ssc)
        E("s", lambda e, ct=ct: e.activation(
            wt[:, 1:1025], xn[0][:, 0:1024], AF.Exp, bias=tdp[:, ct:ct + 1], scale=0.0), ssc)
        W("v", ssc)
        E("v", lambda e, ct=ct: e.tensor_tensor(
            psc[:, 1:1025], qs[:, 1:1025], vcar[ct][:], AL.mult), sv)
        E("v", lambda e, ct=ct: e.tensor_tensor_scan(
            NUMt[:], wt[:], psc[:], mrcv[:, ct:ct + 1], AL.mult, AL.add), sv)
        E("v", lambda e, ct=ct: e.tensor_tensor_scan(
            DENt[:], wt[:], qs[:], mrcv[:, 16 + ct:17 + ct], AL.mult, AL.add), sv)
        E("v", lambda e, ct=ct: e.scalar_tensor_tensor(
            NUMt[:, 0:1024], psc[:, 1:1025], etfp[:, ct:ct + 1], NUMt[:, 0:1024],
            AL.mult, AL.add), sv)
        E("v", lambda e, ct=ct: e.scalar_tensor_tensor(
            DENt[:, 0:1024], qs[:, 1:1025], etfp[:, ct:ct + 1], DENt[:, 0:1024],
            AL.mult, AL.add), sv)
        W("s", sv)
        E("s", lambda e: e.activation(wt[:, 1:1025], DENt[:, 0:1024], AF.Ln), ssc)
        E("s", lambda e: e.activation(
            DENt[:, 0:1024], wt[:, 1:1025], AF.Exp, scale=-1.0), ssc)
        W("v", ssc)
        E("v", lambda e: e.tensor_tensor(
            NUMt[:, 0:1024], NUMt[:, 0:1024], DENt[:, 0:1024], AL.mult), sv)
        E("v", lambda e, ct=ct: e.tensor_tensor(
            ai[ct][:], NUMt[:, 0:1024], ai[ct][:], AL.mult), sv)
        E("v", lambda e: e.memset(wt[:, 0:1], 1.0), sv)
    n_ai = sv.n

    if stage < 6:
        joinall("g")
        for ct in range(16):
            E("g", lambda e, ct=ct: e.dma_start(out=dbg["dbg_ai"][ct], in_=ai[ct][:]), sout, 16)
        E("g", lambda e: e.dma_start(out=dbg["dbg_state"], in_=mrcv[:]), sout, 16)
    if stage == 2:
        W("y", sout, sout.n)
        E("y", lambda e: e.dma_start(out=out_fm[0:128, :], in_=xr[0][:]), sout, 16)
        W("y", sout, sout.n)
        return finish()

    wob = _Slots(2)
    for co in range(16):
        sl = pslot.acq(sched, "g")
        E("g", lambda e, sl=sl, co=co: e.dma_start(
            out=pbuf[sl][:], in_=wo0[co].rearrange("p a b -> p (a b)")), sw, 16)
        W("t", sw)
        W("t", sv, n_ai)
        for tik in range(2):
            bk = wob.acq(sched, "t")
            for ct in range(16):
                E("t", lambda e, sl=sl, ct=ct, bk=bk, tik=tik: e.matmul(
                    pbank[bk][:], pbuf[sl][:, ct * 128:(ct + 1) * 128],
                    ai[ct][:, tik * 512:(tik + 1) * 512],
                    start=(ct == 0), stop=(ct == 15)), smm if ct == 15 else None)
            W("s", smm)
            E("s", lambda e, bk=bk, co=co, tik=tik: e.copy(
                xr[co][:, tik * 512:(tik + 1) * 512], pbank[bk][:]), ssc)
            wob.rel(bk, ssc)
        pslot.rel(sl, smm)
    n_wo_mm = smm.n
    n_wo = ssc.n

    joinall("t", "v", "s", "g")
    dslot = _Slots(2)
    gub = _Slots(4)
    dnb = _Slots(2)
    gsl = _Slots(2)
    for tik in range(2):
        for ig in range(2):
            for ipo in range(32):
                ip = ig * 32 + ipo
                sl = dslot.acq(sched, "g")
                E("g", lambda e, sl=sl, ip=ip: e.dma_start(
                    out=dbuf[sl][:], in_=gu0[ip].rearrange("p a b -> p (a b)")), sw, 16)
                bkg = gub.acq(sched, "t")
                W("t", sw)
                for ci in range(16):
                    E("t", lambda e, sl=sl, ci=ci, bkg=bkg, tik=tik: e.matmul(
                        pbank[bkg][:], dbuf[sl][:, ci * 128:(ci + 1) * 128],
                        xn[ci][:, 1 + tik * 512:1 + tik * 512 + 512],
                        start=(ci == 0), stop=(ci == 15)), smm if ci == 15 else None)
                bku = gub.acq(sched, "t")
                for ci in range(16):
                    E("t", lambda e, sl=sl, ci=ci, bku=bku, tik=tik: e.matmul(
                        pbank[bku][:], dbuf[sl][:, 2048 + ci * 128:2048 + (ci + 1) * 128],
                        xn[ci][:, 1 + tik * 512:1 + tik * 512 + 512],
                        start=(ci == 0), stop=(ci == 15)), smm if ci == 15 else None)
                dslot.rel(sl, smm)
                gl = gsl.acq(sched, "s")
                W("s", smm)
                E("s", lambda e, bkg=bkg, ip=ip, gl=gl: e.activation(
                    gsil[gl][:], pbank[bkg][:], AF.Silu, scale=sg0t[:, ip:ip + 1]), ssc)
                gub.rel(bkg, ssc)
                W("v", ssc)
                E("v", lambda e, bku=bku, ip=ip, ipo=ipo, gl=gl: e.scalar_tensor_tensor(
                    fuse[ipo][:], pbank[bku][:], su0t[:, ip:ip + 1], gsil[gl][:],
                    AL.mult, AL.mult), sv)
                gub.rel(bku, sv)
                gsl.rel(gl, sv)
            n_fuse = sv.n
            for co in range(16):
                sl = dslot.acq(sched, "g")
                E("g", lambda e, sl=sl, co=co, ig=ig: e.dma_start(
                    out=dbuf[sl][:],
                    in_=d0[co][:, ig * 32:(ig + 1) * 32, :].rearrange("p a b -> p (a b)")), sw, 16)
                bk = dnb.acq(sched, "t")
                W("t", sw)
                W("t", sv, n_fuse)
                for j in range(32):
                    E("t", lambda e, sl=sl, j=j, bk=bk: e.matmul(
                        pbank[4 + bk][:], dbuf[sl][:, j * 128:(j + 1) * 128], fuse[j][:],
                        start=(j == 0), stop=(j == 31)), smm if j == 31 else None)
                dslot.rel(sl, smm)
                W("v", smm)
                E("v", lambda e, bk=bk, co=co, tik=tik: e.scalar_tensor_tensor(
                    xr[co][:, tik * 512:(tik + 1) * 512], pbank[4 + bk][:],
                    sd0t[:, co:co + 1], xr[co][:, tik * 512:(tik + 1) * 512],
                    AL.mult, AL.add), sv)
                dnb.rel(bk, sv)
            W("t", sv)

    n_x1 = sgd.n

    if stage < 6:
        joinall("g")
        for ct in range(16):
            E("g", lambda e, ct=ct: e.dma_start(out=dbg["dbg_x1"][ct], in_=xr[ct][:]), sout, 16)
    if stage == 3:
        W("y", sout, sout.n)
        E("y", lambda e: e.dma_start(out=out_fm[0:128, :], in_=xr[0][:]), sout, 16)
        W("y", sout, sout.n)
        return finish()

    joinall("t", "v", "s")
    ch2 = [(0, 512), (512, 1024)]
    W("t", sgd, n_x1)
    W("s", sgd, n_x1)
    for ct in range(16):
        W("s", smm)
        E("s", lambda e, ct=ct: e.activation(xsq[:, 0:1024], xr[ct][:], AF.Square), ssc)
        W("t", ssc)
        for ci, (a, b_) in enumerate(ch2):
            E("t", lambda e, a=a, b_=b_, ci=ci, ct=ct: e.matmul(
                pbank[ci][0:1, :], onf[:], xr[ct][:, a:b_],
                start=(ct == 0), stop=(ct == 15)), smm)
            E("t", lambda e, a=a, b_=b_, ci=ci, ct=ct: e.matmul(
                pbank[2 + ci][0:1, :], onf[:], xsq[:, a:b_],
                start=(ct == 0), stop=(ct == 15)), smm)
    emit_ln_lane([0, 1], [2, 3], ch2)
    W("v", ssc)
    for ct in range(16):
        E("v", lambda e, ct=ct: e.tensor_tensor(
            xa[:, 0:1024], xr[ct][:], rstdB[:, 0:1024], AL.mult), sv)
        E("v", lambda e, ct=ct: e.tensor_tensor(
            xn[ct][:, 0:1024], xa[:, 0:1024], mrsB[:, 0:1024], AL.subtract), sv)
    n_xn1 = sv.n

    if stage >= 4 and stage < 6:
        joinall("g")
        for ct in range(16):
            E("g", lambda e, ct=ct: e.dma_start(
                out=dbg["dbg_xn1"][ct], in_=xn[ct][:, 0:1024]), sout, 16)

    joinall("t", "v", "s", "g")
    W("g", sv, n_xn1)
    for ci in range(16):
        E("g", lambda e, ci=ci: e.dma_start(
            out=wv1sb[:, ci * 512:(ci + 1) * 512], in_=wv1[ci]), sw, 16)
    kqb = _Slots(2)
    vsl = _Slots(2)
    ksl = _Slots(2)
    n_wv1 = sw.n
    for tt in range(8):
        bk = kqb.acq(sched, "t")
        W("t", sw, n_wv1)
        W("t", sv, n_xn1)
        for ci in range(16):
            E("t", lambda e, ci=ci, tt=tt, bk=bk: e.matmul(
                pbank[bk][:], xn[ci][:, tt * 128:(tt + 1) * 128],
                wv1sb[:, ci * 512:(ci + 1) * 512],
                start=(ci == 0), stop=(ci == 15)), smm if ci == 15 else None)
        s_ = vsl.acq(sched, "s")
        W("s", smm)
        E("s", lambda e, bk=bk, s_=s_: e.copy(vstg[s_][:], pbank[bk][:]), ssc)
        kqb.rel(bk, ssc)
        W("y", ssc)
        E("y", lambda e, s_=s_, tt=tt: e.dma_start(out=cc2V_in[tt], in_=vstg[s_][:]), sci, 16)
        vsl.rel(s_, sci)
    for kvh in range(4):
        sl = pslot.acq(sched, "g")
        E("g", lambda e, sl=sl, kvh=kvh: e.dma_start(
            out=pbuf[sl][:], in_=wk1[kvh].rearrange("p a b -> p (a b)")), sw, 16)
        for tik in range(2):
            bk = kqb.acq(sched, "t")
            W("t", sw)
            for ci in range(16):
                E("t", lambda e, sl=sl, ci=ci, bk=bk, tik=tik: e.matmul(
                    pbank[bk][:], pbuf[sl][:, ci * 128:(ci + 1) * 128],
                    xn[ci][:, tik * 512:(tik + 1) * 512],
                    start=(ci == 0), stop=(ci == 15)), smm if ci == 15 else None)
            W("s", smm)
            W("s", sv)
            E("s", lambda e, bk=bk: e.copy(t1[:], pbank[bk][:]), ssc)
            kqb.rel(bk, ssc)
            W("t", ssc)
            W("t", sv)
            E("t", lambda e: e.matmul(pbank[7][:], prm[:], t1[:], start=True, stop=True), smm)
            ks = ksl.acq(sched, "v")
            W("v", smm)
            sl0 = tik * 512
            E("v", lambda e, sl0=sl0: e.tensor_tensor(
                t2[:], t1[:], cosT[:, sl0:sl0 + 512], AL.mult), sv)
            E("v", lambda e, sl0=sl0: e.tensor_tensor(
                t1[:], pbank[7][:], sinT[:, sl0:sl0 + 512], AL.mult), sv)
            E("v", lambda e, ks=ks: e.tensor_tensor(
                kstg[ks][:], t2[:], t1[:], AL.add), sv)
            W("y", sv)
            E("y", lambda e, ks=ks, kvh=kvh, tik=tik: e.dma_start(
                out=cc2K_in[kvh][:, tik * 512:(tik + 1) * 512], in_=kstg[ks][:]), sci, 16)
            ksl.rel(ks, sci)
        pslot.rel(sl, smm)
    W("g", sci)
    E("g", lambda e: e.collective_compute(
        "AllGather", AL.bypass, replica_groups=[[0, 1], [2, 3], [4, 5], [6, 7]],
        ins=[cc2_in], outs=[cc2_out]), scc)
    n_cc2 = scc.n

    joinall("t", "v", "s")

    def emit_ffn1(ig, skip_down=False, ntik=2):
        n0 = ig * 22
        for tik in range(ntik):
            for ipo in range(22):
                ip = n0 + ipo
                sl = dslot.acq(sched, "g")
                E("g", lambda e, sl=sl, ip=ip: e.dma_start(
                    out=dbuf[sl][:], in_=gu1[ip].rearrange("p a b -> p (a b)")), sw, 16)
                bkg = gub.acq(sched, "t")
                W("t", sw)
                for ci in range(16):
                    E("t", lambda e, sl=sl, ci=ci, bkg=bkg, tik=tik: e.matmul(
                        pbank[bkg][:], dbuf[sl][:, ci * 128:(ci + 1) * 128],
                        xn[ci][:, tik * 512:(tik + 1) * 512],
                        start=(ci == 0), stop=(ci == 15)), smm if ci == 15 else None)
                bku = gub.acq(sched, "t")
                for ci in range(16):
                    E("t", lambda e, sl=sl, ci=ci, bku=bku, tik=tik: e.matmul(
                        pbank[bku][:], dbuf[sl][:, 2048 + ci * 128:2048 + (ci + 1) * 128],
                        xn[ci][:, tik * 512:(tik + 1) * 512],
                        start=(ci == 0), stop=(ci == 15)), smm if ci == 15 else None)
                dslot.rel(sl, smm)
                gl = gsl.acq(sched, "s")
                W("s", smm)
                E("s", lambda e, bkg=bkg, ip=ip, gl=gl: e.activation(
                    gsil[gl][:], pbank[bkg][:], AF.Silu, scale=gs1t[:, ip:ip + 1]), ssc)
                gub.rel(bkg, ssc)
                W("v", ssc)
                E("v", lambda e, bku=bku, ip=ip, ipo=ipo, gl=gl: e.scalar_tensor_tensor(
                    fuse[ipo][:], pbank[bku][:], us1t[:, ip:ip + 1], gsil[gl][:],
                    AL.mult, AL.mult), sv)
                gub.rel(bku, sv)
                gsl.rel(gl, sv)
            n_fuse = sv.n
            for co in ([] if skip_down else range(16)):
                sl = dslot.acq(sched, "g")
                E("g", lambda e, sl=sl, co=co, ig=ig: e.dma_start(
                    out=dbuf[sl][:, 0:2816],
                    in_=d1[co][:, ig * 22:(ig + 1) * 22, :].rearrange("p a b -> p (a b)")), sw, 16)
                bk = dnb.acq(sched, "t")
                W("t", sw)
                W("t", sv, n_fuse)
                for j in range(22):
                    E("t", lambda e, sl=sl, j=j, bk=bk: e.matmul(
                        pbank[4 + bk][:], dbuf[sl][:, j * 128:(j + 1) * 128], fuse[j][:],
                        start=(j == 0), stop=(j == 21)), smm if j == 21 else None)
                dslot.rel(sl, smm)
                W("v", smm)
                E("v", lambda e, bk=bk, co=co, tik=tik: e.scalar_tensor_tensor(
                    xr[co][:, tik * 512:(tik + 1) * 512], pbank[4 + bk][:],
                    ds1t[:, co:co + 1], xr[co][:, tik * 512:(tik + 1) * 512],
                    AL.mult, AL.add), sv)
                dnb.rel(bk, sv)
            W("t", sv)

    emit_ffn1(0)

    joinall("t", "v", "s", "y")
    W("y", scc, n_cc2)
    stb = _Slots(3)
    qb_s = _Slots(2)
    pt_s = _Slots(3)
    for kvh in range(4):
        W("y", sv)
        W("y", ssc)
        W("y", smm)
        for r in range(2):
            E("y", lambda e, r=r, kvh=kvh: e.dma_start(
                out=kkv[:, r * 1024:(r + 1) * 1024],
                in_=cc2_out[r, kvh * 131072:(kvh + 1) * 131072].rearrange(
                    "(p n) -> p n", p=128)), sco, 16)
            for tt in range(8):
                E("y", lambda e, r=r, tt=tt, kvh=kvh: e.dma_start(
                    out=vkv[:, (r * 8 + tt) * 128:(r * 8 + tt + 1) * 128],
                    in_=cc2_out[r, 524288 + tt * 65536:524288 + (tt + 1) * 65536].rearrange(
                        "(p n) -> p n", p=128)[:, kvh * 128:(kvh + 1) * 128]), sco, 16)
        n_kv_ld = sco.n
        if stage >= 4 and stage < 6:
            joinall("g")
            E("g", lambda e, kvh=kvh: e.dma_start(out=dbg["dbg_kf"][kvh], in_=kkv[:]), sout, 16)
            if kvh == 0:
                E("g", lambda e: e.dma_start(out=dbg["dbg_vf"], in_=vkv[:]), sout, 16)
        if stage == 4 and kvh == 0:
            W("y", sout, sout.n)
            E("y", lambda e: e.dma_start(out=out_fm[0:128, :], in_=xr[0][:]), sout, 16)
            W("y", sout, sout.n)
            return finish()
        for hh in range(4):
            h = kvh * 4 + hh
            sl = pslot.acq(sched, "g")
            E("g", lambda e, sl=sl, h=h: e.dma_start(
                out=pbuf[sl][:], in_=wq1[h].rearrange("p a b -> p (a b)")), sw, 16)
            qsl = qb_s.acq(sched, "t")
            for tik in range(2):
                bk = kqb.acq(sched, "t")
                W("t", sw)
                W("t", sv, n_xn1)
                for ci in range(16):
                    E("t", lambda e, sl=sl, ci=ci, bk=bk, tik=tik: e.matmul(
                        pbank[bk][:], pbuf[sl][:, ci * 128:(ci + 1) * 128],
                        xn[ci][:, tik * 512:(tik + 1) * 512],
                        start=(ci == 0), stop=(ci == 15)), smm if ci == 15 else None)
                W("s", smm)
                W("s", sv)
                E("s", lambda e, bk=bk: e.copy(t1[:], pbank[bk][:]), ssc)
                kqb.rel(bk, ssc)
                W("t", ssc)
                W("t", sv)
                E("t", lambda e: e.matmul(pbank[7][:], prm[:], t1[:], start=True, stop=True), smm)
                W("v", smm)
                sl0 = tik * 512
                E("v", lambda e, sl0=sl0: e.tensor_tensor(
                    t2[:], t1[:], cosT[:, sl0:sl0 + 512], AL.mult), sv)
                E("v", lambda e, sl0=sl0: e.tensor_tensor(
                    t1[:], pbank[7][:], sinT[:, sl0:sl0 + 512], AL.mult), sv)
                E("v", lambda e, qsl=qsl, sl0=sl0: e.tensor_tensor(
                    qbf[qsl][:, sl0:sl0 + 512], t2[:], t1[:], AL.add), sv)
            pslot.rel(sl, smm)
            n_q = sv.n
            STB = [2, 3, 6]
            lrecs = [lrec, lrec2]
            for qt in range(2):
                band = set(range(4 * qt, 4 * qt + 4)) | set(range(8 + 4 * qt, 8 + 4 * qt + 4))

                def emit_st(kt):
                    bk2 = stb.acq(sched, "t")
                    W("t", sco, n_kv_ld)
                    W("t", sv, n_q)
                    E("t", lambda e, kt=kt, bk2=bk2, qt=qt, qsl=qsl: e.matmul(
                        pbank[STB[bk2]][:], kkv[:, kt * 128:(kt + 1) * 128],
                        qbf[qsl][:, qt * 512:(qt + 1) * 512],
                        start=True, stop=True), smm)
                    return bk2

                bks = {}
                for pre in range(3):
                    bks[pre] = emit_st(pre)
                for kt in range(16):
                    pts = pt_s.acq(sched, "s")
                    W("s", smm)
                    bidx = qt * 16 + kt
                    bkv = bks[kt]
                    E("s", lambda e, bkv=bkv, pts=pts, bidx=bidx: e.activation(
                        ptb[pts][:], pbank[STB[bkv]][:], AF.Exp,
                        bias=btbl[:, bidx:bidx + 1], scale=ROPE_SCALE), ssc)
                    stb.rel(bkv, ssc)
                    n_exp = ssc.n
                    masked = kt in band
                    if masked:
                        slot = (0 if kt < 8 else 4) + (kt % 4)
                        W("v", ssc, n_exp)
                        E("v", lambda e, pts=pts, slot=slot: e.tensor_tensor(
                            ptb[pts][:], ptb[pts][:], bmsk[:, slot * 512:(slot + 1) * 512],
                            AL.mult), sv)
                        n_mask = sv.n
                    if kt + 3 < 16:
                        bks[kt + 3] = emit_st(kt + 3)
                    if masked:
                        W("t", sv, n_mask)
                    else:
                        W("t", ssc, n_exp)
                    if kt == 0:
                        W("t", sv)
                        W("t", ssc)
                    E("t", lambda e, pts=pts, kt=kt, qt=qt: e.matmul(
                        pbank[4 + qt][:], vkv[:, kt * 128:(kt + 1) * 128], ptb[pts][:],
                        start=(kt == 0), stop=(kt == 15)), smm if kt == 15 else None)
                    E("t", lambda e, pts=pts, kt=kt: e.matmul(
                        pbank[7][0:1, :], onbf[:], ptb[pts][:],
                        start=(kt == 0), stop=(kt == 15)), smm if kt == 15 else None)
                    pt_s.rel(pts, smm)
                W("s", smm)
                E("s", lambda e: e.activation(lscr[0:1, :], pbank[7][0:1, :], AF.Ln), ssc)
                E("s", lambda e, qt=qt: e.activation(
                    lrecs[qt][0:1, :], lscr[0:1, :], AF.Exp, scale=-1.0), ssc)
            for qt in range(2):
                W("t", ssc)
                if qt == 1:
                    W("t", sv)
                E("t", lambda e, qt=qt: e.matmul(pbank[7][:], onrow[:],
                                                 lrecs[qt][0:1, :],
                                                 start=True, stop=True), smm)
                W("s", smm)
                if qt == 1:
                    W("s", sv)
                E("s", lambda e: e.copy(lrecB[:], pbank[7][:]), ssc)
                W("v", ssc)
                E("v", lambda e, h=h, qt=qt: e.tensor_tensor(
                    ai[h][:, qt * 512:(qt + 1) * 512], pbank[4 + qt][:], lrecB[:],
                    AL.mult), sv)
            qb_s.rel(qsl, smm)
    n_attn = sv.n

    if stage >= 5 and stage < 6:
        joinall("g")
        for ct in range(16):
            E("g", lambda e, ct=ct: e.dma_start(out=dbg["dbg_on"][ct], in_=ai[ct][:]), sout, 16)
    if stage == 56:
        pass
    if stage == 5:
        W("y", sout, sout.n)
        E("y", lambda e: e.dma_start(out=out_fm[0:128, :], in_=xr[0][:]), sout, 16)
        W("y", sout, sout.n)
        return finish()

    joinall("t", "v", "s")
    for co in ([] if stage == 56 else range(16)):
        sl = pslot.acq(sched, "g")
        E("g", lambda e, sl=sl, co=co: e.dma_start(
            out=pbuf[sl][:], in_=wo1[co].rearrange("p a b -> p (a b)")), sw, 16)
        W("t", sw)
        W("t", sv, n_attn)
        for tik in range(2):
            bk = wob.acq(sched, "t")
            for ht in range(16):
                E("t", lambda e, sl=sl, ht=ht, bk=bk, tik=tik: e.matmul(
                    pbank[bk][:], pbuf[sl][:, ht * 128:(ht + 1) * 128],
                    ai[ht][:, tik * 512:(tik + 1) * 512],
                    start=(ht == 0), stop=(ht == 15)), smm if ht == 15 else None)
            W("v", smm)
            E("v", lambda e, bk=bk, co=co, tik=tik: e.tensor_tensor(
                xr[co][:, tik * 512:(tik + 1) * 512],
                xr[co][:, tik * 512:(tik + 1) * 512], pbank[bk][:], AL.add), sv)
            wob.rel(bk, sv)
        pslot.rel(sl, smm)

    if stage != 55:
        joinall("t", "v", "s", "g")
        if stage == 57:
            emit_ffn1(1, skip_down=True)
        elif stage == 58:
            emit_ffn1(1, ntik=1)
        else:
            emit_ffn1(1)

    joinall("y")
    for ct in range(16):
        E("y", lambda e, ct=ct: e.dma_start(
            out=out_fm[ct * 128:(ct + 1) * 128, :], in_=xr[ct][:]), sout, 16)
    W("y", sout, sout.n)
    return finish()


def _build_and_run(inputs, stage=6):
    maps = _host_inputs(inputs)
    nc, es, dbg = build_nc(stage)
    try:
        res = run_bass_kernel_spmd(nc, maps, core_ids=list(range(8)))
    finally:
        es.close()
    return res


def kernel(**inputs):
    res = _build_and_run(inputs, stage=6)
    out = np.empty((B, T, H), np.float32)
    for c in range(8):
        b, s = c // 2, c % 2
        out[b, s * HALF:(s + 1) * HALF, :] = np.asarray(res.results[c]["out_fm"]).T
    return out
